# version 14
# baseline (speedup 1.0000x reference)
import numpy as np
import ml_dtypes
import concourse.bass as bass
import concourse.mybir as mybir
from concourse.bass_utils import run_bass_kernel_spmd

F32 = mybir.dt.float32
BF16 = mybir.dt.bfloat16
AF = mybir.ActivationFunctionType
ALU = mybir.AluOpType
AX = mybir.AxisListType

ENGINES = ("tensor", "vector", "scalar", "sync", "gpsimd")
DMA_INC = {"sync": 16, "gpsimd": 16}

N = 2048
D = 1024
DK = 1024
DV = 2048
NH = 4
HQK = 256
HV = 512
NT = 16
NSLAB = 4
SLAB = 512
NE = 8
DF = 2048
LN16 = float(np.log(16.0))


class Buf:
    __slots__ = ("last_w", "readers", "name", "excl")

    def __init__(self, name="", excl=False):
        self.name = name
        self.last_w = None
        self.readers = {}
        self.excl = excl


class Sched:

    def __init__(self):
        self.ops = {e: [] for e in ENGINES}
        self.dep_targets = {e: set() for e in ENGINES}
        self.pending_barrier = {e: [] for e in ENGINES}
        self._uniq = 0

    def op(self, eng, fn, reads=(), writes=(), deps=(), wkey=None):
        reads = list(reads)
        writes = list(writes)
        for b in list(reads):
            if b.excl:
                reads.remove(b)
                writes.append(b)
        idx = len(self.ops[eng])
        dd = {}

        def add(t):
            e2, i2 = t
            if e2 == eng and eng == "tensor":
                return
            k2 = self.ops[e2][i2][2]
            if dd.get((e2, k2), -1) < i2:
                dd[(e2, k2)] = i2

        for t in self.pending_barrier[eng]:
            if not (t[0] == eng and t[1] >= idx):
                add(t)
        self.pending_barrier[eng] = []
        for b in reads:
            if b.last_w is not None:
                add(b.last_w)
        for b in writes:
            if b.last_w is not None:
                add(b.last_w)
            for e2, i2 in b.readers.items():
                add((e2, i2))
        for t in deps:
            if t is not None:
                add(t)
        if eng in DMA_INC:
            if wkey is None:
                if writes:
                    wkey = writes[0].name
                else:
                    self._uniq += 1
                    wkey = "u%d" % self._uniq
        else:
            wkey = eng
        self.ops[eng].append((fn, dd, wkey))
        for (e2, _k2), i2 in dd.items():
            self.dep_targets[e2].add(i2)
        me = (eng, idx)
        for b in reads:
            if b.readers.get(eng, -1) < idx:
                b.readers[eng] = idx
        for b in writes:
            b.last_w = me
            b.readers = {}
        return me

    def barrier(self):
        lasts = [(e, len(self.ops[e]) - 1) for e in ENGINES if self.ops[e]]
        for e in ENGINES:
            self.pending_barrier[e] = list(lasts)

    def lower(self, nc):
        incof = {e: DMA_INC.get(e, 1) for e in ENGINES}
        opsem = {e: [None] * len(self.ops[e]) for e in ENGINES}
        keys = []
        keyset = set()
        for e in ENGINES:
            kc = {}
            for i, (fn, dd, wkey) in enumerate(self.ops[e]):
                has_inc = (e in DMA_INC) or (i in self.dep_targets[e])
                if has_inc:
                    kc[wkey] = kc.get(wkey, 0) + incof[e]
                    opsem[e][i] = (wkey, kc[wkey])
                    if wkey not in keyset:
                        keyset.add(wkey)
                        keys.append(wkey)
                else:
                    opsem[e][i] = None
        import contextlib
        with contextlib.ExitStack() as st:
            sems = {k: st.enter_context(nc.semaphore("sem_" + k)) for k in keys}
            with nc.Block() as block:
                for ename in ENGINES:
                    ops = self.ops[ename]

                    def body(eng, ename=ename, ops=ops):
                        waited = {}
                        for i, (fn, dd, wkey) in enumerate(ops):
                            for e2, _k2 in sorted(dd):
                                tgt = opsem[e2][dd[(e2, _k2)]]
                                assert tgt is not None, (ename, i, e2, _k2)
                                k, v = tgt
                                if waited.get(k, 0) < v:
                                    eng.wait_ge(sems[k], v)
                                    waited[k] = v
                            inst = fn(eng)
                            mysem = opsem[ename][i]
                            if mysem is not None:
                                inst.then_inc(sems[mysem[0]], incof[ename])

                    getattr(block, ename)(body)


def build_program(core_b):
    nc = bass.Bass("TRN2", target_bir_lowering=False)
    sch = Sched()

    def dram(name, shape, dtype, kind="ExternalInput"):
        return nc.dram_tensor(name, shape, dtype, kind=kind).ap()

    x_d = dram("xTt", [128, 8, N], F32)
    wqT_d = dram("WqTt", [8, 128, 8, 128], F32)
    wkT_d = dram("WkTt", [8, 128, 8, 128], F32)
    wvT_d = dram("WvTt", [8, 128, 8, 256], F32)
    wgT_d = dram("WgTt", [8, 128, 8, 256], F32)
    wgk1T_d = dram("Wgk1Tt", [128, 8, 32], F32)
    wgk2Tb_d = dram("Wgk2Tb", [33, DK], F32)
    woutT_d = dram("WoutTt", [8, 2, 128, 8, 128], F32)
    routerT_d = dram("routerTt", [128, 64], F32)
    w1_d = dram("w1t", [128, 128, 8, 128], F32)
    w2_d = dram("w2t", [8, 8, 128, 16, 128], F32)
    ct_d = dram("CTt", [128, 64], F32)
    wmod_d = dram("wmod", [D, 6 * D], F32)
    rmsw_d = dram("rmswRow", [1, HV], F32)
    consb_d = dram("consBf", [128, 512], BF16)
    onesrb_d = dram("onesRowBf", [1, 128], BF16)
    onesrf_d = dram("onesRowF", [1, 128], F32)
    identf_d = dram("identF", [128, 128], F32)
    selmat_d = dram("selMat", [8, 1024], BF16)
    cfl_d = dram("cflC", [128, 8], F32)
    y_d = dram("y", [D, N], F32, kind="ExternalOutput")

    import contextlib
    with contextlib.ExitStack() as st:
        sb = lambda nm, shape, dt: st.enter_context(nc.sbuf_tensor(nm, shape, dt))
        ps = lambda nm, shape, dt: st.enter_context(nc.psum_tensor(nm, shape, dt))

        xT = sb("xTsb", [128, 8 * N], F32)
        hbuf = sb("hbuf", [128, 8192], BF16)
        S = sb("S", [128, 8 * HV], F32)
        S_bf = sb("S_bf", [128, 8 * HV], BF16)
        bigbuf = sb("bigbuf", [128, 16384], BF16)
        ogbuf = sb("ogbuf", [128, 8192], BF16)
        vslab = sb("vslab", [128, 8192], BF16)
        gslab = sb("gslab", [128, 8192], BF16)
        wvg = sb("wvg", [128, 2 * 2048], BF16)
        wm = sb("wm", [128, 1024], BF16)
        scratch = sb("scratch", [128, 512], F32)
        scratch2 = sb("scratch2", [128, 512], F32)
        rowsB = sb("rowsB", [1, 512], F32)
        consb = sb("consb", [128, 512], BF16)
        onesrb = sb("onesrb", [1, 128], BF16)
        onesrf = sb("onesrf", [1, 128], F32)
        identf = sb("identf", [128, 128], F32)
        rmswbc = sb("rmswbc", [128, HV], BF16)
        cfl = sb("cfl", [128, 8], F32)
        siluC = sb("siluC", [128, 64], BF16)
        wgk1T = sb("wgk1T", [128, 8 * 32], BF16)
        wgk2Tb = sb("wgk2Tb", [33, DK], BF16)
        routerT = sb("routerT", [128, 64], BF16)
        modc = sb("modc", [128, 64], F32)
        eblastT = sb("eblastT", [128, 8 * 16], F32)
        ltok = sb("ltok", [128, 16 * 8], F32)
        redc = sb("redc", [128, 8], F32)
        gsc = sb("gsc", [128, 128], F32)

        modr = scratch[:, 0:64]
        rowsA = scratch[0:1, 0:512]
        xbf_s = [vslab[:, 0:512], vslab[:, 1024:1536]]
        x2bf_s = [vslab[:, 512:1024], vslab[:, 1536:2048]]
        innerT = bigbuf[0:33, 14976:15488]
        zrow = consb[0:1, 448:512]
        selmat = gslab[0:8, 0:1024]
        logitsF = gslab[0:8, 1024:1536]
        gateTf = gslab[0:8, 2048:4096]
        gate_bc = vslab[:, 2048:4096]

        p0 = ps("p0", [128, 512], F32)
        p1 = ps("p1", [128, 512], F32)
        p2 = ps("p2", [128, 512], F32)
        p3 = ps("p3", [128, 512], F32)
        p4 = ps("p4", [128, 512], F32)
        p5 = ps("p5", [128, 512], F32)
        pT = ps("pT", [128, 1024], BF16)
        pU = ps("pU", [128, 1024], BF16)

        UTneg = consb[:, 0:128]
        UTmask = consb[:, 128:256]
        identb = consb[:, 256:384]
        ones_col = consb[:, 384:385]

        B = {}

        def buf(nm):
            if nm not in B:
                B[nm] = Buf(nm)
            return B[nm]

        for _pn in ("p0", "p1", "p2", "p3", "p4", "p5", "pT", "pU"):
            B[_pn] = Buf(_pn, excl=True)
        pb = {k: B[k] for k in ("p0", "p1", "p2", "p3", "p4", "p5", "pT", "pU")}

        def T(fn, reads=(), writes=(), deps=()):
            return sch.op("tensor", fn, reads, writes, deps)

        def V(fn, reads=(), writes=(), deps=()):
            return sch.op("vector", fn, reads, writes, deps)

        def C(fn, reads=(), writes=(), deps=()):
            return sch.op("scalar", fn, reads, writes, deps)

        def SY(fn, reads=(), writes=(), deps=(), wkey=None):
            return sch.op("sync", fn, reads, writes, deps, wkey=wkey)

        def G(fn, reads=(), writes=(), deps=()):
            return sch.op("gpsimd", fn, reads, writes, deps)

        SY(lambda e: e.dma_start(out=consb[:, :], in_=consb_d), writes=[buf("consb")])
        SY(lambda e: e.dma_start(out=onesrb[:, :], in_=onesrb_d), writes=[buf("onesrb")])
        SY(lambda e: e.dma_start(out=onesrf[:, :], in_=onesrf_d), writes=[buf("onesrf")])
        SY(lambda e: e.dma_start(out=identf[:, :], in_=identf_d), writes=[buf("identf")])
        SY(lambda e: e.dma_start(out=rowsB[0:1, 0:HV], in_=rmsw_d), writes=[buf("rowsB")])
        SY(lambda e: e.dma_start(out=modr, in_=ct_d), writes=[buf("scratch")])
        SY(lambda e: e.dma_start(out=cfl[:, :], in_=cfl_d), writes=[buf("cfl")])
        SY(lambda e: e.dma_start(
            out=xT[:, :].rearrange("p (kt t) -> p kt t", kt=8), in_=x_d),
            writes=[buf("xT")])
        G(lambda e: e.dma_start(out=wgk1T[:, :].rearrange("p (kt m) -> p kt m", kt=8),
                                in_=wgk1T_d), writes=[buf("wgk1T")])
        G(lambda e: e.dma_start(out=wgk2Tb[:, :], in_=wgk2Tb_d), writes=[buf("wgk2Tb")])
        G(lambda e: e.dma_start(out=routerT[:, :], in_=routerT_d), writes=[buf("routerT")])

        T(lambda e: e.matmul(p5[:, 0:HV], onesrf[0:1, :], rowsB[0:1, 0:HV],
                             start=True, stop=True),
          reads=[buf("onesrf"), buf("rowsB")], writes=[pb["p5"]])
        V(lambda e: e.tensor_copy(out=rmswbc[:, :], in_=p5[:, 0:HV]),
          reads=[pb["p5"]], writes=[buf("rmswbc")])

        C(lambda e: e.activation(out=siluC[:, :], in_=modr, func=AF.Silu),
          reads=[buf("scratch")], writes=[buf("siluC")])

        wmst = [bigbuf[:, 0:6144], bigbuf[:, 6144:12288]]
        wmst_b = [buf("wmst0"), buf("wmst1")]
        T(lambda e: e.matmul(p4[:, 0:48], onesrb[0:1, 0:128], zrow[:, 0:48],
                             start=True, stop=False),
          reads=[buf("onesrb"), buf("consb")], writes=[pb["p4"]])
        for k in range(8):
            bsel = k % 2
            G(lambda e, k=k, bsel=bsel: e.dma_start(
                out=wmst[bsel].rearrange("p (c m) -> p c m", c=48),
                in_=wmod_d[k * 128:(k + 1) * 128, :].rearrange("p (c m) -> p c m", c=48)),
                writes=[wmst_b[bsel]])
            for cc in range(48):
                T(lambda e, k=k, bsel=bsel, cc=cc: e.matmul(
                    p4[:, cc:cc + 1], wmst[bsel][:, cc * 128:(cc + 1) * 128],
                    siluC[:, k * 8 + core_b:k * 8 + core_b + 1],
                    start=False, stop=(k == 7 and cc == 47)),
                    reads=[wmst_b[bsel], buf("siluC")], writes=[pb["p4"]])
        V(lambda e: e.tensor_copy(out=modr[:, 0:48], in_=p4[:, 0:48]),
          reads=[pb["p4"]], writes=[buf("scratch")])
        V(lambda e: e.tensor_copy(out=modc[:, 0:8], in_=modr[:, 0:8]),
          reads=[buf("scratch")], writes=[buf("modc")])
        V(lambda e: e.tensor_scalar_add(modc[:, 8:16], modr[:, 8:16], 1.0),
          reads=[buf("scratch")], writes=[buf("modc")])
        V(lambda e: e.tensor_copy(out=modc[:, 24:32], in_=modr[:, 24:32]),
          reads=[buf("scratch")], writes=[buf("modc")])
        V(lambda e: e.tensor_scalar_add(modc[:, 32:40], modr[:, 32:40], 1.0),
          reads=[buf("scratch")], writes=[buf("modc")])
        for src, dst in ((16, 16), (40, 40)):
            V(lambda e, src=src: e.tensor_single_scalar(
                out=modr[:, 48:56], in_=modr[:, src:src + 8], scalar=0.0, op=ALU.is_ge),
                reads=[buf("scratch")], writes=[buf("scratch")])
            V(lambda e: e.tensor_scalar(
                out=modr[:, 48:56], in0=modr[:, 48:56], scalar1=2.0, scalar2=-1.0,
                op0=ALU.mult, op1=ALU.add),
                reads=[buf("scratch")], writes=[buf("scratch")])
            V(lambda e, src=src: e.tensor_mul(modr[:, 56:64], modr[:, src:src + 8],
                                              modr[:, 48:56]),
                reads=[buf("scratch")], writes=[buf("scratch")])
            V(lambda e: e.tensor_scalar_max(modr[:, 56:64], modr[:, 56:64], 0.25),
                reads=[buf("scratch")], writes=[buf("scratch")])
            V(lambda e, dst=dst: e.tensor_mul(modc[:, dst:dst + 8], modr[:, 56:64],
                                              modr[:, 48:56]),
                reads=[buf("scratch")], writes=[buf("modc")])

        sch.barrier()

        qe_sb = bigbuf[:, 0:4096]
        kd_sb = bigbuf[:, 4096:8192]
        gtok_sb = bigbuf[:, 8192:12288]
        ebc_sb = [bigbuf[:, 12288:12800], bigbuf[:, 13312:13824]]
        emb_sb = [bigbuf[:, 12800:13312], bigbuf[:, 13824:14336]]
        kdecf_sb = bigbuf[:, 14336:14592]
        kdect_sb = bigbuf[:, 14592:14848]
        atm_sb = bigbuf[:, 14848:14976]
        bq = [buf("qe%d" % m) for m in range(8)]
        bk = [buf("kd%d" % m) for m in range(8)]
        bg = [buf("gtok%d" % t4) for t4 in range(4)]
        bebc = [buf("ebc0"), buf("ebc1")]
        bemb = [buf("emb0"), buf("emb1")]
        bkdecf = buf("kdecf")
        bkdect = buf("kdect")
        batm = buf("atm")
        bS = [buf("S%d" % h) for h in range(NH)]
        bSbf = [buf("Sbf%d" % h) for h in range(NH)]
        bxT = [buf("xTs%d" % s) for s in range(NSLAB)]
        bh = buf("hbuf")
        bv = buf("vslab")
        bgt = buf("gslab")
        bogf = buf("ogf")
        beb = buf("eblastT")
        bwvg = [buf("wvg0"), buf("wvg1")]
        bwm = [buf("wm0"), buf("wm1")]
        binner = buf("innerT")
        bltok = buf("ltok")
        blogF = buf("gslab")
        bscr = buf("scratch")
        bscr2n = buf("scratch2")
        brA = buf("scratch")
        brB = buf("rowsB")
        bredc = buf("redc")
        bgTf = buf("gslab")
        bgbc = buf("vslab")
        bxbf_s = [buf("xbf0"), buf("xbf1")]
        bx2bf_s = [buf("x2bf0"), buf("x2bf1")]

        wm_pool = [[wm[:, 0:1024],
                    hbuf[:, 4096:5120], hbuf[:, 5120:6144],
                    hbuf[:, 6144:7168], hbuf[:, 7168:8192]]]
        bwm_pool = [[bwm[0], buf("wmx0"), buf("wmx1"), buf("wmx2"), buf("wmx3")]]
        wvg_half = [wvg[:, 0:2048], wvg[:, 2048:4096]]
        wm_ctr = [0]
        wvg_ctr = [0]

        def stream_lhsT_chunk(dram4, ci):
            pool = wm_pool[0]
            bsel = wm_ctr[0] % len(pool)
            wm_ctr[0] += 1
            dst = pool[bsel]
            bb = bwm_pool[0][bsel]
            G(lambda e: e.dma_start(
                out=dst[:, :].rearrange("p (kt m) -> p kt m", kt=8),
                in_=dram4[ci]),
                writes=[bb])
            return dst, bb

        wvg_pool = [[wvg[:, 0:2048], wvg[:, 2048:4096]]]
        bwvg_pool = [[bwvg[0], bwvg[1]]]

        def stream_wvg_chunk(dram4, ci, kt, m):
            pool = wvg_pool[0]
            bsel = wvg_ctr[0] % len(pool)
            wvg_ctr[0] += 1
            dst = pool[bsel]
            bb = bwvg_pool[0][bsel]
            G(lambda e: e.dma_start(
                out=dst[:, :].rearrange("p (kt m) -> p kt m", kt=kt),
                in_=dram4[ci]),
                writes=[bb])
            return dst, bb

        def ln_slab(s, scale_off, bias_off, dst_c0, dst_stride, pA=None, pAb=None,
                    pB=None, pBb=None):
            if pA is None:
                pA, pAb, pB, pBb = p0, pb["p0"], p1, pb["p1"]
            c0 = s * SLAB
            for d in range(8):
                sl2 = d % 2
                V(lambda e, d=d, c0=c0, sl2=sl2: e.tensor_copy(
                    out=xbf_s[sl2], in_=xT[:, d * N + c0: d * N + c0 + SLAB]),
                    reads=[bxT[s]], writes=[bxbf_s[sl2], bv])
                C(lambda e, d=d, c0=c0, sl2=sl2: e.activation(
                    out=x2bf_s[sl2], in_=xT[:, d * N + c0: d * N + c0 + SLAB],
                    func=AF.Square),
                    reads=[bxT[s]], writes=[bx2bf_s[sl2], bv])
                T(lambda e, d=d, sl2=sl2: e.matmul(pA[0:1, :], ones_col, xbf_s[sl2],
                                                   start=(d == 0), stop=(d == 7)),
                  reads=[bxbf_s[sl2], buf("consb")], writes=[pAb])
                T(lambda e, d=d, sl2=sl2: e.matmul(pB[0:1, :], ones_col, x2bf_s[sl2],
                                                   start=(d == 0), stop=(d == 7)),
                  reads=[bx2bf_s[sl2], buf("consb")], writes=[pBb])
            V(lambda e: e.tensor_single_scalar(out=rowsA, in_=pA[0:1, :],
                                               scalar=1.0 / D, op=ALU.mult),
              reads=[pAb], writes=[brA])
            V(lambda e: e.tensor_tensor(out=rowsB[0:1, :], in0=rowsA, in1=rowsA,
                                        op=ALU.mult),
              reads=[brA], writes=[brB])
            V(lambda e: e.scalar_tensor_tensor(out=rowsB[0:1, :], in0=pB[0:1, :],
                                               scalar=1.0 / D, in1=rowsB[0:1, :],
                                               op0=ALU.mult, op1=ALU.subtract),
              reads=[pBb, brB], writes=[brB])
            C(lambda e: e.activation(out=rowsB[0:1, :], in_=rowsB[0:1, :], func=AF.Sqrt,
                                     bias=cfl[0:1, 0:1]),
              reads=[brB, buf("cfl")], writes=[brB])
            V(lambda e: e.reciprocal(out=rowsB[0:1, :], in_=rowsB[0:1, :]),
              reads=[brB], writes=[brB])
            T(lambda e: e.matmul(pA[:, :], onesrf[0:1, :], rowsA, start=True, stop=True),
              reads=[buf("onesrf"), brA], writes=[pAb])
            T(lambda e: e.matmul(pB[:, :], onesrf[0:1, :], rowsB[0:1, :], start=True, stop=True),
              reads=[buf("onesrf"), brB], writes=[pBb])
            for d in range(8):
                sl2 = d % 2
                scr_s = scratch[:, :] if sl2 == 0 else scratch2[:, :]
                bsc_s = bscr if sl2 == 0 else bscr2n
                V(lambda e, d=d, c0=c0, scr_s=scr_s: e.scalar_tensor_tensor(
                    out=scr_s, in0=pA[:, :], scalar=-1.0,
                    in1=xT[:, d * N + c0: d * N + c0 + SLAB], op0=ALU.mult, op1=ALU.add),
                    reads=[pAb, bxT[s]], writes=[bsc_s])
                V(lambda e, d=d, scr_s=scr_s: e.tensor_tensor(
                    out=scr_s, in0=scr_s, in1=pB[:, :], op=ALU.mult),
                    reads=[bsc_s, pBb], writes=[bsc_s])
                C(lambda e, d=d, dst_c0=dst_c0, dst_stride=dst_stride, scr_s=scr_s: e.activation(
                    out=hbuf[:, d * dst_stride + dst_c0: d * dst_stride + dst_c0 + SLAB],
                    in_=scr_s, func=AF.Identity,
                    scale=modc[:, scale_off + d:scale_off + d + 1],
                    bias=modc[:, bias_off + d:bias_off + d + 1]),
                    reads=[bsc_s, buf("modc")], writes=[bh])

        qk_pool = [(p0, pb["p0"]), (p1, pb["p1"]), (p4, pb["p4"]), (p5, pb["p5"])]

        def proj_slab_featmaj(wdram, m, pbank, pbuf):
            wb, wbb = stream_lhsT_chunk(wdram, m)
            for k in range(8):
                T(lambda e, k=k, wb=wb, pbank=pbank: e.matmul(
                    pbank[:, :], wb[:, k * 128:(k + 1) * 128],
                    hbuf[:, k * 512:(k + 1) * 512],
                    start=(k == 0), stop=(k == 7)),
                    reads=[wbb, bh], writes=[pbuf])

        for s in range(NSLAB):
            c0 = s * SLAB
            ln_slab(s, 8, 0, 0, 512, p2, pb["p2"], p3, pb["p3"])
            for k in range(8):
                T(lambda e, k=k: e.matmul(p0[0:32, :], wgk1T[:, k * 32:(k + 1) * 32],
                                          hbuf[:, k * 512:(k + 1) * 512],
                                          start=(k == 0), stop=(k == 7)),
                  reads=[buf("wgk1T"), bh], writes=[pb["p0"]])
            V(lambda e: e.tensor_copy(out=innerT[0:32, 0:512], in_=p0[0:32, :]),
              reads=[pb["p0"]], writes=[binner])
            if s == 0:
                V(lambda e: e.memset(innerT[32:33, 0:512], 1.0), writes=[binner])
            for t4 in range(4):
                for nn in range(2):
                    gp, gpb = (p1, pb["p1"]) if (t4 * 2 + nn) % 2 == 0 else (p4, pb["p4"])
                    T(lambda e, t4=t4, nn=nn, gp=gp: e.matmul(
                        gp[:, :], innerT[:, t4 * 128:(t4 + 1) * 128],
                        wgk2Tb[:, nn * 512:(nn + 1) * 512], start=True, stop=True),
                        reads=[binner, buf("wgk2Tb")], writes=[gpb])
                    C(lambda e, t4=t4, nn=nn, gp=gp: e.activation(
                        out=gtok_sb[:, t4 * 1024 + nn * 512: t4 * 1024 + (nn + 1) * 512],
                        in_=gp[:, :], func=AF.Sigmoid),
                        reads=[gpb], writes=[bg[t4]])
            for t4 in range(4):
                C(lambda e, t4=t4: e.activation(
                    out=gtok_sb[:, t4 * 1024: (t4 + 1) * 1024],
                    in_=gtok_sb[:, t4 * 1024: (t4 + 1) * 1024], func=AF.Ln),
                    reads=[bg[t4]], writes=[bg[t4]])
            for dp in range(4):
                for j, pbk, pbf in ((0, p2, pb["p2"]), (1, p3, pb["p3"])):
                    d = 2 * dp + j
                    for t4 in range(4):
                        T(lambda e, d=d, t4=t4, pbk=pbk: e.matmul(
                            pbk[:, t4 * 128:(t4 + 1) * 128],
                            gtok_sb[:, t4 * 1024 + d * 128: t4 * 1024 + (d + 1) * 128],
                            UTneg, start=True, stop=True),
                            reads=[bg[t4], buf("consb")], writes=[pbf])
                    C(lambda e, j=j, pbk=pbk: e.activation(out=ebc_sb[j][:, :], in_=pbk[:, :],
                                                           func=AF.Exp, bias=cfl[:, 1:2]),
                      reads=[pbf, buf("cfl")], writes=[bebc[j]])
                    C(lambda e, j=j, pbk=pbk: e.activation(out=emb_sb[j][:, :], in_=pbk[:, :],
                                                           func=AF.Exp, scale=cfl[:, 2:3]),
                      reads=[pbf, buf("cfl")], writes=[bemb[j]])
                    V(lambda e, d=d, j=j, s=s: e.tensor_single_scalar(
                        out=eblastT[:, d * 16 + s * 4: d * 16 + s * 4 + 4]
                        .rearrange("p (c m) -> p c m", m=1),
                        in_=ebc_sb[j].rearrange("p (c m) -> p c m", m=128)[:, :, 127:128],
                        scalar=16.0, op=ALU.mult),
                        reads=[bebc[j]], writes=[beb])
                for j in (0, 1):
                    d = 2 * dp + j
                    qp, qpb = qk_pool[(2 * d) % 4]
                    proj_slab_featmaj(wqT_d, d, qp, qpb)
                    V(lambda e, d=d, j=j, qp=qp: e.tensor_tensor(
                        out=qe_sb[:, d * 512:(d + 1) * 512], in0=qp[:, :], in1=ebc_sb[j][:, :],
                        op=ALU.mult),
                        reads=[qpb, bebc[j]], writes=[bq[d]])
                    kp, kpb = qk_pool[(2 * d + 1) % 4]
                    proj_slab_featmaj(wkT_d, d, kp, kpb)
                    V(lambda e, d=d, j=j, kp=kp: e.tensor_tensor(
                        out=kd_sb[:, d * 512:(d + 1) * 512], in0=kp[:, :], in1=emb_sb[j][:, :],
                        op=ALU.mult),
                        reads=[kpb, bemb[j]], writes=[bk[d]])
            vg_pool = [(p2, pb["p2"]), (p3, pb["p3"]), (p0, pb["p0"]), (p1, pb["p1"])]
            vg_bufs = [ogbuf[:, 0:4096], ogbuf[:, 4096:8192]]
            for half, wdram_, dstbuf, dstb, actf in (
                    (0, wvT_d, vslab, bv, None), (1, wgT_d, gslab, bgt, AF.Silu)):
                for nv in range(4):
                    wb = vg_bufs[nv % 2]
                    wbb = bogf
                    for c_ in range(2):
                        G(lambda e, nv=nv, c_=c_, wb=wb, wdram_=wdram_: e.dma_start(
                            out=wb.rearrange("p (kt cm) -> p kt cm", kt=8)
                            [:, :, c_ * 256:(c_ + 1) * 256],
                            in_=wdram_[nv * 2 + c_]),
                            writes=[wbb])
                    for t4 in range(4):
                        pbk, pbf = vg_pool[(nv * 4 + t4) % 4]
                        for k in range(8):
                            T(lambda e, k=k, t4=t4, wb=wb, pbk=pbk: e.matmul(
                                pbk[:, :],
                                hbuf[:, k * 512 + t4 * 128: k * 512 + t4 * 128 + 128],
                                wb[:, k * 512:(k + 1) * 512],
                                start=(k == 0), stop=(k == 7)),
                                reads=[wbb, bh], writes=[pbf])
                        if actf is None:
                            V(lambda e, nv=nv, t4=t4, pbk=pbk, dstbuf=dstbuf: e.tensor_copy(
                                out=dstbuf[:, t4 * 2048 + nv * 512: t4 * 2048 + (nv + 1) * 512],
                                in_=pbk[:, :]),
                                reads=[pbf], writes=[dstb])
                        else:
                            C(lambda e, nv=nv, t4=t4, pbk=pbk, dstbuf=dstbuf, actf=actf:
                              e.activation(
                                  out=dstbuf[:, t4 * 2048 + nv * 512: t4 * 2048 + (nv + 1) * 512],
                                  in_=pbk[:, :], func=actf),
                                reads=[pbf], writes=[dstb])
                            V(lambda e, nv=nv, t4=t4, dstbuf=dstbuf: e.tensor_tensor(
                                out=dstbuf[:, t4 * 2048 + nv * 512: t4 * 2048 + (nv + 1) * 512],
                                in0=dstbuf[:, t4 * 2048 + nv * 512: t4 * 2048 + (nv + 1) * 512],
                                in1=rmswbc[:, 0:512], op=ALU.mult),
                                reads=[dstb, buf("rmswbc")], writes=[dstb])
            wvg_pool[0] = [wvg[:, 0:2048], wvg[:, 2048:4096]]
            bwvg_pool[0] = [bwvg[0], bwvg[1]]
            pend_tr = []
            for t4 in range(4):
                tau = s * 4 + t4
                vt = vslab[:, t4 * 2048: (t4 + 1) * 2048]
                gt = gslab[:, t4 * 2048: (t4 + 1) * 2048]
                for h in range(NH):
                    while len(pend_tr) > 3:
                        pend_tr.pop(0)()
                    qcols = [(2 * h + j) * 512 + t4 * 128 for j in (0, 1)]
                    if tau > 0:
                        for j in (0, 1):
                            T(lambda e, j=j, qcols=qcols, h=h: e.matmul(
                                p4[:, :], qe_sb[:, qcols[j]:qcols[j] + 128],
                                S_bf[:, (2 * h + j) * 512:(2 * h + j + 1) * 512],
                                start=(j == 0), stop=False),
                                reads=[bq[2 * h + j], bSbf[h]], writes=[pb["p4"]])
                    for j in (0, 1):
                        T(lambda e, j=j, qcols=qcols: e.matmul(
                            p5[:, 0:128], kd_sb[:, qcols[j]:qcols[j] + 128],
                            qe_sb[:, qcols[j]:qcols[j] + 128],
                            start=(j == 0), stop=(j == 1)),
                            reads=[bk[2 * h + j], bq[2 * h + j]], writes=[pb["p5"]])
                    V(lambda e: e.tensor_tensor(out=atm_sb[:, :], in0=p5[:, 0:128],
                                                in1=UTmask, op=ALU.mult),
                      reads=[pb["p5"], buf("consb")], writes=[batm])
                    T(lambda e, h=h, tau=tau, vt=vt: e.matmul(
                        p4[:, :], atm_sb[:, :], vt[:, h * 512:(h + 1) * 512],
                        start=(tau == 0), stop=True),
                        reads=[batm, bv], writes=[pb["p4"]])
                    for j in (0, 1):
                        T_eb = eblastT[:, (2 * h + j) * 16 + tau: (2 * h + j) * 16 + tau + 1]
                        C(lambda e, j=j, qcols=qcols, T_eb=T_eb: e.mul(
                            out=kdecf_sb[:, j * 128:(j + 1) * 128],
                            in_=kd_sb[:, qcols[j]:qcols[j] + 128], mul=T_eb),
                            reads=[bk[2 * h + j], beb], writes=[bkdecf])
                        T(lambda e, j=j: e.transpose(
                            pT[:, j * 128:(j + 1) * 128], kdecf_sb[:, j * 128:(j + 1) * 128],
                            identb),
                            reads=[bkdecf, buf("consb")], writes=[pb["pT"]])
                    C(lambda e: e.copy(out=kdect_sb[:, :], in_=pT[:, 0:256]),
                      reads=[pb["pT"]], writes=[bkdect])
                    for j, pbk, pbf in ((0, p2, pb["p2"]), (1, p3, pb["p3"])):
                        T(lambda e, j=j, h=h, pbk=pbk, vt=vt: e.matmul(
                            pbk[:, :], kdect_sb[:, j * 128:(j + 1) * 128],
                            vt[:, h * 512:(h + 1) * 512], start=True, stop=True),
                            reads=[bkdect, bv], writes=[pbf])
                    C(lambda e: e.activation(out=scratch[:, :], in_=p4[:, :], func=AF.Square,
                                             accum_out=redc[:, 0:1]),
                      reads=[pb["p4"]], writes=[bscr, bredc])
                    C(lambda e: e.activation(out=redc[:, 1:2], in_=redc[:, 0:1], func=AF.Sqrt,
                                             scale=cfl[:, 4:5], bias=cfl[:, 0:1]),
                      reads=[bredc, buf("cfl")], writes=[bredc])
                    V(lambda e: e.reciprocal(out=redc[:, 2:3], in_=redc[:, 1:2]),
                      reads=[bredc], writes=[bredc])
                    C(lambda e: e.activation(out=scratch[:, :], in_=p4[:, :], func=AF.Identity,
                                             scale=redc[:, 2:3]),
                      reads=[pb["p4"], bredc], writes=[bscr])
                    V(lambda e, h=h, gt=gt: e.tensor_tensor(
                        out=gt[:, h * 512:(h + 1) * 512], in0=scratch[:, :],
                        in1=gt[:, h * 512:(h + 1) * 512], op=ALU.mult),
                        reads=[bscr, bgt], writes=[bgt])

                    def emit_og_transposes(h=h, t4=t4, gt=gt):
                        pp, ppb = pU, pb["pU"]
                        for ci in range(4):
                            cc = h * 4 + ci
                            T(lambda e, cc=cc, ci=ci, pp=pp, gt=gt: e.transpose(
                                pp[:, ci * 128:(ci + 1) * 128],
                                gt[:, cc * 128:(cc + 1) * 128], identb),
                                reads=[bgt, buf("consb")], writes=[ppb])
                        V(lambda e, h=h, t4=t4, pp=pp: e.tensor_copy(
                            out=ogbuf[:, :].rearrange("p (c m) -> p c m", m=512)
                            [:, h * 4:(h + 1) * 4, t4 * 128:(t4 + 1) * 128],
                            in_=pp[:, 0:512].rearrange("p (c m) -> p c m", m=128)),
                            reads=[ppb], writes=[bogf])
                    pend_tr.append(emit_og_transposes)
                    for j, pbk, pbf in ((0, p2, pb["p2"]), (1, p3, pb["p3"])):
                        scol = (2 * h + j) * 512
                        T_eb = eblastT[:, (2 * h + j) * 16 + tau: (2 * h + j) * 16 + tau + 1]
                        if tau == 0:
                            V(lambda e, scol=scol, pbk=pbk: e.tensor_copy(
                                out=S[:, scol:scol + 512], in_=pbk[:, :]),
                                reads=[pbf], writes=[bS[h]])
                        else:
                            V(lambda e, scol=scol, pbk=pbk, T_eb=T_eb: e.scalar_tensor_tensor(
                                out=S[:, scol:scol + 512], in0=S[:, scol:scol + 512],
                                scalar=T_eb, in1=pbk[:, :], op0=ALU.mult, op1=ALU.add),
                                reads=[bS[h], beb, pbf], writes=[bS[h]])
                    C(lambda e, h=h: e.copy(out=S_bf[:, h * 1024:(h + 1) * 1024],
                                            in_=S[:, h * 1024:(h + 1) * 1024]),
                      reads=[bS[h]], writes=[bSbf[h]])


            while pend_tr:
                pend_tr.pop(0)()
            for m in range(8):
                for kh in (0, 1):
                    wb, wbb = stream_lhsT_chunk(woutT_d[m], kh)
                    for k in range(8):
                        kk = kh * 8 + k
                        T(lambda e, kk=kk, k=k, wb=wb: e.matmul(
                            p0[:, :], wb[:, k * 128:(k + 1) * 128],
                            ogbuf[:, kk * 512:(kk + 1) * 512],
                            start=(kk == 0), stop=(kk == 15)),
                            reads=[wbb, bogf], writes=[pb["p0"]])
                V(lambda e, m=m, c0=c0, s=s: e.scalar_tensor_tensor(
                    out=xT[:, m * N + c0: m * N + c0 + SLAB], in0=p0[:, :],
                    scalar=modc[:, 16 + m:16 + m + 1],
                    in1=xT[:, m * N + c0: m * N + c0 + SLAB], op0=ALU.mult, op1=ALU.add),
                    reads=[pb["p0"], buf("modc"), bxT[s]], writes=[bxT[s]])

        sch.barrier()
        w1big = [vslab[:, 4096:6144], vslab[:, 6144:8192]]
        bw1big = [buf("wmf0"), buf("wmf1")]
        wm_pool[0] = [wm[:, 0:1024]]
        bwm_pool[0] = [bwm[0]]

        SY(lambda e: e.dma_start(out=selmat, in_=selmat_d), writes=[buf("selmat")],
           deps=())
        for th in range(2):
            for sl2 in range(2):
                ln_slab(th * 2 + sl2, 32, 24, sl2 * 512, 1024, p2, pb["p2"], p3, pb["p3"])
            for sl2 in range(2):
                for k in range(8):
                    T(lambda e, k=k, sl2=sl2: e.matmul(
                        p1[0:8, :], routerT[:, k * 8:(k + 1) * 8],
                        hbuf[:, k * 1024 + sl2 * 512: k * 1024 + sl2 * 512 + 512],
                        start=(k == 0), stop=(k == 7)),
                        reads=[buf("routerT"), bh], writes=[pb["p1"]])
                V(lambda e: e.tensor_copy(out=logitsF, in_=p1[0:8, :]),
                  reads=[pb["p1"]], writes=[blogF])
                for t4 in range(4):
                    T(lambda e, t4=t4: e.transpose(
                        pT[:, 0:8], logitsF[:, t4 * 128:(t4 + 1) * 128], identb[0:8, 0:8]),
                        reads=[blogF, buf("consb")], writes=[pb["pT"]])
                    tt = th * 8 + sl2 * 4 + t4
                    V(lambda e, tt=tt: e.tensor_copy(out=ltok[:, tt * 8:(tt + 1) * 8],
                                                     in_=pT[:, 0:8]),
                      reads=[pb["pT"]], writes=[bltok])
            def emit_gating(th=th):
                for tt in range(th * 8, th * 8 + 8):
                    lsl = ltok[:, tt * 8:(tt + 1) * 8]
                    r = gsc[:, (tt % 4) * 32:(tt % 4) * 32 + 32]
                    bt = buf("gsc%d" % (tt % 4))
                    m1 = r[:, 24:25]
                    m2 = r[:, 25:26]
                    d21 = r[:, 26:27]
                    den = r[:, 27:28]
                    rc_ = r[:, 28:29]
                    V(lambda e, lsl=lsl, m1=m1: e.reduce_max(out=m1, in_=lsl, axis=AX.X),
                      reads=[bltok], writes=[bt])
                    V(lambda e, lsl=lsl, r=r, m1=m1: e.tensor_single_scalar(
                        out=r[:, 0:8], in_=lsl, scalar=m1, op=ALU.subtract),
                      reads=[bltok, bt], writes=[bt])
                    C(lambda e, r=r: e.activation(out=r[:, 8:16], in_=r[:, 0:8], func=AF.Exp),
                      reads=[bt], writes=[bt])
                    V(lambda e, lsl=lsl, r=r, m1=m1: e.tensor_single_scalar(
                        out=r[:, 16:24], in_=lsl, scalar=m1, op=ALU.is_ge),
                      reads=[bltok, bt], writes=[bt])
                    V(lambda e, lsl=lsl, r=r: e.scalar_tensor_tensor(
                        out=r[:, 16:24], in0=r[:, 16:24], scalar=-1e30,
                        in1=lsl, op0=ALU.mult, op1=ALU.add),
                      reads=[bt, bltok], writes=[bt])
                    V(lambda e, r=r, m2=m2: e.reduce_max(out=m2, in_=r[:, 16:24], axis=AX.X),
                      reads=[bt], writes=[bt])
                    V(lambda e, m1=m1, m2=m2, d21=d21: e.tensor_tensor(
                        out=d21, in0=m2, in1=m1, op=ALU.subtract),
                      reads=[bt], writes=[bt])
                    C(lambda e, d21=d21, den=den: e.activation(out=den, in_=d21, func=AF.Exp),
                      reads=[bt], writes=[bt])
                    V(lambda e, den=den: e.tensor_scalar_add(den, den, 1.0),
                      reads=[bt], writes=[bt])
                    V(lambda e, den=den, rc_=rc_: e.reciprocal(out=rc_, in_=den),
                      reads=[bt], writes=[bt])
                    V(lambda e, lsl=lsl, r=r, m2=m2: e.tensor_single_scalar(
                        out=r[:, 0:8], in_=lsl, scalar=m2, op=ALU.is_ge),
                      reads=[bltok, bt], writes=[bt])
                    V(lambda e, r=r, rc_=rc_: e.tensor_single_scalar(
                        out=r[:, 8:16], in_=r[:, 8:16], scalar=rc_, op=ALU.mult),
                      reads=[bt], writes=[bt])
                    V(lambda e, tt=tt, r=r: e.tensor_tensor(
                        out=ltok[:, tt * 8:(tt + 1) * 8], in0=r[:, 8:16],
                        in1=r[:, 0:8], op=ALU.mult),
                      reads=[bt], writes=[bltok])
                for tt in range(th * 8, th * 8 + 8):
                    T(lambda e, tt=tt: e.transpose(p2[0:8, 0:128], ltok[:, tt * 8:(tt + 1) * 8],
                                                   identf[:, :]),
                      reads=[bltok, buf("identf")], writes=[pb["p2"]])
                    V(lambda e, tt=tt: e.tensor_copy(out=gateTf[:, tt * 128:(tt + 1) * 128],
                                                     in_=p2[0:8, 0:128]),
                      reads=[pb["p2"]], writes=[bgTf])

            w1_pool = [(p0, pb["p0"]), (p1, pb["p1"]), (p2, pb["p2"]), (p3, pb["p3"])]
            w2_pool = [(p4, pb["p4"]), (p5, pb["p5"]), (p2, pb["p2"]), (p3, pb["p3"])]
            for e_i in range(NE):
                def emit_gate_bc(e_i=e_i, th=th):
                    for sl in range(2):
                        T(lambda e, e_i=e_i, sl=sl, th=th: e.matmul(
                            p4[:, :], selmat[:, e_i * 128:(e_i + 1) * 128],
                            gateTf[:, th * 1024 + sl * 512: th * 1024 + (sl + 1) * 512],
                            start=True, stop=True),
                            reads=[buf("selmat"), bgTf], writes=[pb["p4"]])
                        V(lambda e, sl=sl: e.tensor_copy(
                            out=gate_bc[:, sl * 512:(sl + 1) * 512], in_=p4[:, :]),
                          reads=[pb["p4"]], writes=[bgbc])
                defer = (e_i == 0)
                pending_mults = []
                if not defer:
                    emit_gate_bc()
                for m2 in range(8):
                    if defer and m2 == 4:
                        emit_gating()
                        emit_gate_bc()
                        for pm in pending_mults:
                            pm()
                        pending_mults = []
                        defer = False
                    bsel = m2 % 2
                    wb, wbb = w1big[bsel], bw1big[bsel]
                    G(lambda e, e_i=e_i, m2=m2, wb=wb: e.dma_start(
                        out=wb.rearrange("p (mm kt m) -> p mm kt m", mm=2, kt=8),
                        in_=w1_d[e_i * 16 + m2 * 2: e_i * 16 + m2 * 2 + 2]
                        .rearrange("mm p kt m -> p mm kt m")),
                        writes=[wbb])
                    for mi in range(2):
                        m = m2 * 2 + mi
                        for k in range(8):
                            for n2 in range(2):
                                pbk, pbf = w1_pool[(m % 2) * 2 + n2]
                                T(lambda e, k=k, n2=n2, mi=mi, wb=wb, pbk=pbk: e.matmul(
                                    pbk[:, :], wb[:, mi * 1024 + k * 128: mi * 1024 + (k + 1) * 128],
                                    hbuf[:, k * 1024 + n2 * 512: k * 1024 + n2 * 512 + 512],
                                    start=(k == 0), stop=(k == 7)),
                                    reads=[wbb, bh], writes=[pbf])
                        for n2 in range(2):
                            pbk, pbf = w1_pool[(m % 2) * 2 + n2]
                            C(lambda e, m=m, n2=n2, pbk=pbk: e.activation(
                                out=bigbuf[:, m * 1024 + n2 * 512: m * 1024 + (n2 + 1) * 512],
                                in_=pbk[:, :], func=AF.Gelu),
                                reads=[pbf], writes=[buf("he")])
                        def emit_mult(m=m):
                            V(lambda e, m=m: e.tensor_tensor(
                                out=bigbuf[:, m * 1024: (m + 1) * 1024],
                                in0=bigbuf[:, m * 1024: (m + 1) * 1024],
                                in1=gate_bc[:, 0:1024], op=ALU.mult),
                                reads=[buf("he"), bgbc], writes=[buf("he")])
                        if defer:
                            pending_mults.append(emit_mult)
                        else:
                            emit_mult()
                wvg_pool[0] = [wvg[:, 0:2048], wvg[:, 2048:4096],
                               gslab[:, 4096:6144], gslab[:, 6144:8192]]
                bwvg_pool[0] = [bwvg[0], bwvg[1], buf("w2x0"), buf("w2x1")]
                for md in range(8):
                    wb, wbb = stream_wvg_chunk(w2_d[e_i], md, 16, 128)
                    for k in range(16):
                        for n2 in range(2):
                            pbk, pbf = w2_pool[(md % 2) * 2 + n2]
                            T(lambda e, k=k, n2=n2, wb=wb, pbk=pbk: e.matmul(
                                pbk[:, :], wb[:, k * 128:(k + 1) * 128],
                                bigbuf[:, k * 1024 + n2 * 512: k * 1024 + n2 * 512 + 512],
                                start=(k == 0), stop=(k == 15)),
                                reads=[wbb, buf("he")], writes=[pbf])
                    for n2 in range(2):
                        pbk, pbf = w2_pool[(md % 2) * 2 + n2]
                        tc0 = th * 1024 + n2 * 512
                        sx = th * 2 + n2
                        V(lambda e, md=md, tc0=tc0, pbk=pbk, sx=sx: e.scalar_tensor_tensor(
                            out=xT[:, md * N + tc0: md * N + tc0 + 512], in0=pbk[:, :],
                            scalar=modc[:, 40 + md:40 + md + 1],
                            in1=xT[:, md * N + tc0: md * N + tc0 + 512],
                            op0=ALU.mult, op1=ALU.add),
                            reads=[pbf, buf("modc"), bxT[sx]], writes=[bxT[sx]])

        store = SY(lambda e: e.dma_start(
            out=y_d.rearrange("(kt p) t -> p kt t", p=128),
            in_=xT[:, :].rearrange("p (kt t) -> p kt t", kt=8)),
            reads=[bxT[0], bxT[1], bxT[2], bxT[3]])
        C(lambda e: e.nop(), deps=[store])

        sch.lower(nc)
    return nc


_CACHE = {}


def kernel(**inp):
    f32 = np.float32
    x = np.asarray(inp["L0_attn_x"], dtype=f32)
    Wq = np.asarray(inp["L0_attn_Wq"], dtype=f32)
    Wk = np.asarray(inp["L0_attn_Wk"], dtype=f32)
    Wv = np.asarray(inp["L0_attn_Wv"], dtype=f32)
    Wg = np.asarray(inp["L0_attn_Wg"], dtype=f32)
    Wgk1 = np.asarray(inp["L0_attn_Wgk1"], dtype=f32)
    Wgk2 = np.asarray(inp["L0_attn_Wgk2"], dtype=f32)
    bgk2 = np.asarray(inp["L0_attn_bgk2"], dtype=f32)
    Wout = np.asarray(inp["L0_attn_Wout"], dtype=f32)
    rms_w = np.asarray(inp["L0_attn_rms_w"], dtype=f32)
    router_w = np.asarray(inp["L0_ffn_router_w"], dtype=f32)
    w1 = np.asarray(inp["L0_ffn_w1"], dtype=f32)
    w2 = np.asarray(inp["L0_ffn_w2"], dtype=f32)
    Cc = np.asarray(inp["L0_c"], dtype=f32)
    wmod = np.asarray(inp["L0_wmod"], dtype=f32)

    cc = np.ascontiguousarray
    WqTt = cc(Wq.T.reshape(8, 128, 8, 128).transpose(2, 1, 0, 3))
    WkTt = cc(Wk.T.reshape(8, 128, 8, 128).transpose(2, 1, 0, 3))
    WvTt = cc(Wv.T.reshape(8, 128, 8, 256).transpose(2, 1, 0, 3))
    WgTt = cc(Wg.T.reshape(8, 128, 8, 256).transpose(2, 1, 0, 3))
    Wgk1Tt = cc(Wgk1.T.reshape(8, 128, 32).transpose(1, 0, 2))
    Wgk2Tb = cc(np.concatenate([Wgk2.T, bgk2[None, :]], axis=0))
    WoutTt = cc(Wout.T.reshape(2, 8, 128, 8, 128).transpose(3, 0, 2, 1, 4))
    w1t = cc(w1.reshape(8, 128, 128, 128).transpose(2, 1, 0, 3))
    w2t = cc(w2.reshape(8, 16, 128, 8, 128).transpose(0, 3, 2, 1, 4))
    routerTt = cc(router_w.T.reshape(8, 128, 8).transpose(1, 0, 2).reshape(128, 64))
    CTt = cc(Cc.T.reshape(8, 128, 8).transpose(1, 0, 2).reshape(128, 64))
    rmswRow = cc(rms_w[None, :])

    bf = ml_dtypes.bfloat16
    UTneg = np.triu(np.full((128, 128), 1.0 / 32.0, dtype=f32)).astype(bf)
    UTmask = np.triu(np.ones((128, 128), dtype=f32)).astype(bf)
    identb = np.eye(128, dtype=f32).astype(bf)
    consBf = np.zeros((128, 512), dtype=bf)
    consBf[:, 0:128] = UTneg
    consBf[:, 128:256] = UTmask
    consBf[:, 256:384] = identb
    consBf[:, 384] = np.ones(128, dtype=bf)
    onesRowBf = np.ones((1, 128), dtype=bf)
    onesRowF = np.ones((1, 128), dtype=f32)
    identF = np.eye(128, dtype=f32)
    selMat = np.zeros((8, 1024), dtype=bf)
    for e_ in range(8):
        selMat[e_, e_ * 128:(e_ + 1) * 128] = 1.0
    cflC = np.zeros((128, 8), dtype=f32)
    cflC[:, 0] = 1e-5
    cflC[:, 1] = -np.log(16.0)
    cflC[:, 2] = -1.0
    cflC[:, 3] = 1.0 / 1024.0
    cflC[:, 4] = 1.0 / 512.0
    cflC[:, 5] = 16.0
    cflC[:, 6] = 0.25
    cflC[:, 7] = -1e30

    shared = {
        "WqTt": WqTt, "WkTt": WkTt, "WvTt": WvTt, "WgTt": WgTt, "Wgk1Tt": Wgk1Tt,
        "Wgk2Tb": Wgk2Tb, "WoutTt": WoutTt, "routerTt": routerTt, "w1t": w1t, "w2t": w2t,
        "CTt": CTt, "wmod": wmod, "rmswRow": rmswRow, "consBf": consBf,
        "onesRowBf": onesRowBf, "onesRowF": onesRowF, "identF": identF,
        "selMat": selMat, "cflC": cflC,
    }
    in_maps = []
    for b in range(8):
        m = dict(shared)
        m["xTt"] = cc(x[b].T.reshape(8, 128, 2048).transpose(1, 0, 2))
        ct = CTt.reshape(128, 8, 8)
        m["CTt"] = cc(np.concatenate([ct[:, :, b:b + 1], ct[:, :, b + 1:], ct[:, :, :b]],
                                     axis=2).reshape(128, 64))
        in_maps.append(m)

    nc = _CACHE.get("nc")
    if nc is None:
        nc = build_program(0)
        _CACHE["nc"] = nc

    res = run_bass_kernel_spmd(nc, in_maps, core_ids=list(range(8)))
    out = np.empty((8, N, D), dtype=np.float32)
    for b in range(8):
        out[b] = np.asarray(res.results[b]["y"]).T
    return out


# version 15
# speedup vs baseline: 1.1043x; 1.1043x over previous
import numpy as np
import ml_dtypes
import concourse.bass as bass
import concourse.mybir as mybir
from concourse.bass_utils import run_bass_kernel_spmd

F32 = mybir.dt.float32
BF16 = mybir.dt.bfloat16
AF = mybir.ActivationFunctionType
ALU = mybir.AluOpType
AX = mybir.AxisListType

ENGINES = ("tensor", "vector", "scalar", "sync", "gpsimd")
DMA_INC = {"sync": 16, "gpsimd": 16}

N = 2048
D = 1024
DK = 1024
DV = 2048
NH = 4
HQK = 256
HV = 512
NT = 16
NSLAB = 4
SLAB = 512
NE = 8
DF = 2048
LN16 = float(np.log(16.0))


class Buf:
    __slots__ = ("last_w", "readers", "name", "excl")

    def __init__(self, name="", excl=False):
        self.name = name
        self.last_w = None
        self.readers = {}
        self.excl = excl


class Sched:

    def __init__(self):
        self.ops = {e: [] for e in ENGINES}
        self.dep_targets = {e: set() for e in ENGINES}
        self.pending_barrier = {e: [] for e in ENGINES}
        self._uniq = 0

    def op(self, eng, fn, reads=(), writes=(), deps=(), wkey=None):
        reads = list(reads)
        writes = list(writes)
        for b in list(reads):
            if b.excl:
                reads.remove(b)
                writes.append(b)
        idx = len(self.ops[eng])
        dd = {}

        def add(t):
            e2, i2 = t
            if e2 == eng and eng == "tensor":
                return
            k2 = self.ops[e2][i2][2]
            if dd.get((e2, k2), -1) < i2:
                dd[(e2, k2)] = i2

        for t in self.pending_barrier[eng]:
            if not (t[0] == eng and t[1] >= idx):
                add(t)
        self.pending_barrier[eng] = []
        for b in reads:
            if b.last_w is not None:
                add(b.last_w)
        for b in writes:
            if b.last_w is not None:
                add(b.last_w)
            for e2, i2 in b.readers.items():
                add((e2, i2))
        for t in deps:
            if t is not None:
                add(t)
        if eng in DMA_INC:
            if wkey is None:
                if writes:
                    wkey = writes[0].name
                else:
                    self._uniq += 1
                    wkey = "u%d" % self._uniq
        else:
            wkey = eng
        self.ops[eng].append((fn, dd, wkey))
        for (e2, _k2), i2 in dd.items():
            self.dep_targets[e2].add(i2)
        me = (eng, idx)
        for b in reads:
            if b.readers.get(eng, -1) < idx:
                b.readers[eng] = idx
        for b in writes:
            b.last_w = me
            b.readers = {}
        return me

    def barrier(self):
        lasts = [(e, len(self.ops[e]) - 1) for e in ENGINES if self.ops[e]]
        for e in ENGINES:
            self.pending_barrier[e] = list(lasts)

    def lower(self, nc):
        incof = {e: DMA_INC.get(e, 1) for e in ENGINES}
        opsem = {e: [None] * len(self.ops[e]) for e in ENGINES}
        keys = []
        keyset = set()
        for e in ENGINES:
            kc = {}
            for i, (fn, dd, wkey) in enumerate(self.ops[e]):
                has_inc = (e in DMA_INC) or (i in self.dep_targets[e])
                if has_inc:
                    kc[wkey] = kc.get(wkey, 0) + incof[e]
                    opsem[e][i] = (wkey, kc[wkey])
                    if wkey not in keyset:
                        keyset.add(wkey)
                        keys.append(wkey)
                else:
                    opsem[e][i] = None
        import contextlib
        with contextlib.ExitStack() as st:
            sems = {k: st.enter_context(nc.semaphore("sem_" + k)) for k in keys}
            with nc.Block() as block:
                for ename in ENGINES:
                    ops = self.ops[ename]

                    def body(eng, ename=ename, ops=ops):
                        waited = {}
                        for i, (fn, dd, wkey) in enumerate(ops):
                            for e2, _k2 in sorted(dd):
                                tgt = opsem[e2][dd[(e2, _k2)]]
                                assert tgt is not None, (ename, i, e2, _k2)
                                k, v = tgt
                                if waited.get(k, 0) < v:
                                    eng.wait_ge(sems[k], v)
                                    waited[k] = v
                            inst = fn(eng)
                            mysem = opsem[ename][i]
                            if mysem is not None:
                                inst.then_inc(sems[mysem[0]], incof[ename])

                    getattr(block, ename)(body)


def build_program(core_b):
    nc = bass.Bass("TRN2", target_bir_lowering=False)
    sch = Sched()

    def dram(name, shape, dtype, kind="ExternalInput"):
        return nc.dram_tensor(name, shape, dtype, kind=kind).ap()

    x_d = dram("xTt", [128, 8, N], F32)
    wqT_d = dram("WqTt", [8, 128, 8, 128], F32)
    wkT_d = dram("WkTt", [8, 128, 8, 128], F32)
    wvT_d = dram("WvTt", [8, 128, 8, 256], F32)
    wgT_d = dram("WgTt", [8, 128, 8, 256], F32)
    wgk1T_d = dram("Wgk1Tt", [128, 8, 32], F32)
    wgk2Tb_d = dram("Wgk2Tb", [33, DK], F32)
    woutT_d = dram("WoutTt", [8, 2, 128, 8, 128], F32)
    routerT_d = dram("routerTt", [128, 64], F32)
    w1_d = dram("w1t", [128, 128, 8, 128], F32)
    w2_d = dram("w2t", [8, 8, 128, 16, 128], F32)
    ct_d = dram("CTt", [128, 64], F32)
    wmod_d = dram("wmod", [D, 6 * D], F32)
    rmsw_d = dram("rmswRow", [1, HV], F32)
    consb_d = dram("consBf", [128, 512], BF16)
    onesrb_d = dram("onesRowBf", [1, 128], BF16)
    onesrf_d = dram("onesRowF", [1, 128], F32)
    identf_d = dram("identF", [128, 128], F32)
    selmat_d = dram("selMat", [8, 1024], BF16)
    cfl_d = dram("cflC", [128, 8], F32)
    y_d = dram("y", [D, N], F32, kind="ExternalOutput")

    import contextlib
    with contextlib.ExitStack() as st:
        sb = lambda nm, shape, dt: st.enter_context(nc.sbuf_tensor(nm, shape, dt))
        ps = lambda nm, shape, dt: st.enter_context(nc.psum_tensor(nm, shape, dt))

        xT = sb("xTsb", [128, 8 * N], F32)
        hbuf = sb("hbuf", [128, 8192], BF16)
        S = sb("S", [128, 8 * HV], F32)
        S_bf = sb("S_bf", [128, 8 * HV], BF16)
        bigbuf = sb("bigbuf", [128, 16384], BF16)
        ogbuf = sb("ogbuf", [128, 8192], BF16)
        vslab = sb("vslab", [128, 8192], BF16)
        gslab = sb("gslab", [128, 8192], BF16)
        wvg = sb("wvg", [128, 2 * 2048], BF16)
        wm = sb("wm", [128, 1024], BF16)
        scratch = sb("scratch", [128, 512], F32)
        scratch2 = sb("scratch2", [128, 512], F32)
        rowsB = sb("rowsB", [1, 512], F32)
        consb = sb("consb", [128, 512], BF16)
        onesrb = sb("onesrb", [1, 128], BF16)
        onesrf = sb("onesrf", [1, 128], F32)
        identf = sb("identf", [128, 128], F32)
        rmswbc = sb("rmswbc", [128, HV], BF16)
        cfl = sb("cfl", [128, 8], F32)
        siluC = sb("siluC", [128, 64], BF16)
        wgk1T = sb("wgk1T", [128, 8 * 32], BF16)
        wgk2Tb = sb("wgk2Tb", [33, DK], BF16)
        routerT = sb("routerT", [128, 64], BF16)
        modc = sb("modc", [128, 64], F32)
        eblastT = sb("eblastT", [128, 8 * 16], F32)
        ltok = sb("ltok", [128, 16 * 8], F32)
        redc = sb("redc", [128, 8], F32)
        gsc = sb("gsc", [128, 128], F32)

        modr = scratch[:, 0:64]
        rowsA = scratch[0:1, 0:512]
        xbf_s = [vslab[:, 0:512], vslab[:, 1024:1536]]
        x2bf_s = [vslab[:, 512:1024], vslab[:, 1536:2048]]
        innerT = bigbuf[0:33, 14976:15488]
        zrow = consb[0:1, 448:512]
        selmat = gslab[0:8, 0:1024]
        logitsF = gslab[0:8, 1024:1536]
        gateTf = gslab[0:8, 2048:4096]
        gate_bc = vslab[:, 2048:4096]

        p0 = ps("p0", [128, 512], F32)
        p1 = ps("p1", [128, 512], F32)
        p2 = ps("p2", [128, 512], F32)
        p3 = ps("p3", [128, 512], F32)
        p4 = ps("p4", [128, 512], F32)
        p5 = ps("p5", [128, 512], F32)
        pT = ps("pT", [128, 1024], BF16)
        pU = ps("pU", [128, 1024], BF16)

        UTneg = consb[:, 0:128]
        UTmask = consb[:, 128:256]
        identb = consb[:, 256:384]
        ones_col = consb[:, 384:385]

        B = {}

        def buf(nm):
            if nm not in B:
                B[nm] = Buf(nm)
            return B[nm]

        for _pn in ("p0", "p1", "p2", "p3", "p4", "p5", "pT", "pU"):
            B[_pn] = Buf(_pn, excl=True)
        pb = {k: B[k] for k in ("p0", "p1", "p2", "p3", "p4", "p5", "pT", "pU")}

        def T(fn, reads=(), writes=(), deps=()):
            return sch.op("tensor", fn, reads, writes, deps)

        def V(fn, reads=(), writes=(), deps=()):
            return sch.op("vector", fn, reads, writes, deps)

        def C(fn, reads=(), writes=(), deps=()):
            return sch.op("scalar", fn, reads, writes, deps)

        def SY(fn, reads=(), writes=(), deps=(), wkey=None):
            return sch.op("sync", fn, reads, writes, deps, wkey=wkey)

        def G(fn, reads=(), writes=(), deps=()):
            return sch.op("gpsimd", fn, reads, writes, deps)

        SY(lambda e: e.dma_start(out=consb[:, :], in_=consb_d), writes=[buf("consb")])
        SY(lambda e: e.dma_start(out=onesrb[:, :], in_=onesrb_d), writes=[buf("onesrb")])
        SY(lambda e: e.dma_start(out=onesrf[:, :], in_=onesrf_d), writes=[buf("onesrf")])
        SY(lambda e: e.dma_start(out=identf[:, :], in_=identf_d), writes=[buf("identf")])
        SY(lambda e: e.dma_start(out=rowsB[0:1, 0:HV], in_=rmsw_d), writes=[buf("rowsB")])
        SY(lambda e: e.dma_start(out=modr, in_=ct_d), writes=[buf("scratch")])
        SY(lambda e: e.dma_start(out=cfl[:, :], in_=cfl_d), writes=[buf("cfl")])
        SY(lambda e: e.dma_start(
            out=xT[:, :].rearrange("p (kt t) -> p kt t", kt=8), in_=x_d),
            writes=[buf("xT")])
        G(lambda e: e.dma_start(out=wgk1T[:, :].rearrange("p (kt m) -> p kt m", kt=8),
                                in_=wgk1T_d), writes=[buf("wgk1T")])
        G(lambda e: e.dma_start(out=wgk2Tb[:, :], in_=wgk2Tb_d), writes=[buf("wgk2Tb")])
        G(lambda e: e.dma_start(out=routerT[:, :], in_=routerT_d), writes=[buf("routerT")])

        T(lambda e: e.matmul(p5[:, 0:HV], onesrf[0:1, :], rowsB[0:1, 0:HV],
                             start=True, stop=True),
          reads=[buf("onesrf"), buf("rowsB")], writes=[pb["p5"]])
        V(lambda e: e.tensor_copy(out=rmswbc[:, :], in_=p5[:, 0:HV]),
          reads=[pb["p5"]], writes=[buf("rmswbc")])

        C(lambda e: e.activation(out=siluC[:, :], in_=modr, func=AF.Silu),
          reads=[buf("scratch")], writes=[buf("siluC")])

        wmst = [bigbuf[:, 0:6144], bigbuf[:, 6144:12288]]
        wmst_b = [buf("wmst0"), buf("wmst1")]
        T(lambda e: e.matmul(p4[:, 0:48], onesrb[0:1, 0:128], zrow[:, 0:48],
                             start=True, stop=False),
          reads=[buf("onesrb"), buf("consb")], writes=[pb["p4"]])
        for k in range(8):
            bsel = k % 2
            G(lambda e, k=k, bsel=bsel: e.dma_start(
                out=wmst[bsel].rearrange("p (c m) -> p c m", c=48),
                in_=wmod_d[k * 128:(k + 1) * 128, :].rearrange("p (c m) -> p c m", c=48)),
                writes=[wmst_b[bsel]])
            for cc in range(48):
                T(lambda e, k=k, bsel=bsel, cc=cc: e.matmul(
                    p4[:, cc:cc + 1], wmst[bsel][:, cc * 128:(cc + 1) * 128],
                    siluC[:, k * 8 + core_b:k * 8 + core_b + 1],
                    start=False, stop=(k == 7 and cc == 47)),
                    reads=[wmst_b[bsel], buf("siluC")], writes=[pb["p4"]])
        V(lambda e: e.tensor_copy(out=modr[:, 0:48], in_=p4[:, 0:48]),
          reads=[pb["p4"]], writes=[buf("scratch")])
        V(lambda e: e.tensor_copy(out=modc[:, 0:8], in_=modr[:, 0:8]),
          reads=[buf("scratch")], writes=[buf("modc")])
        V(lambda e: e.tensor_scalar_add(modc[:, 8:16], modr[:, 8:16], 1.0),
          reads=[buf("scratch")], writes=[buf("modc")])
        V(lambda e: e.tensor_copy(out=modc[:, 24:32], in_=modr[:, 24:32]),
          reads=[buf("scratch")], writes=[buf("modc")])
        V(lambda e: e.tensor_scalar_add(modc[:, 32:40], modr[:, 32:40], 1.0),
          reads=[buf("scratch")], writes=[buf("modc")])
        for src, dst in ((16, 16), (40, 40)):
            V(lambda e, src=src: e.tensor_single_scalar(
                out=modr[:, 48:56], in_=modr[:, src:src + 8], scalar=0.0, op=ALU.is_ge),
                reads=[buf("scratch")], writes=[buf("scratch")])
            V(lambda e: e.tensor_scalar(
                out=modr[:, 48:56], in0=modr[:, 48:56], scalar1=2.0, scalar2=-1.0,
                op0=ALU.mult, op1=ALU.add),
                reads=[buf("scratch")], writes=[buf("scratch")])
            V(lambda e, src=src: e.tensor_mul(modr[:, 56:64], modr[:, src:src + 8],
                                              modr[:, 48:56]),
                reads=[buf("scratch")], writes=[buf("scratch")])
            V(lambda e: e.tensor_scalar_max(modr[:, 56:64], modr[:, 56:64], 0.25),
                reads=[buf("scratch")], writes=[buf("scratch")])
            V(lambda e, dst=dst: e.tensor_mul(modc[:, dst:dst + 8], modr[:, 56:64],
                                              modr[:, 48:56]),
                reads=[buf("scratch")], writes=[buf("modc")])

        sch.barrier()

        qe_sb = bigbuf[:, 0:4096]
        kd_sb = bigbuf[:, 4096:8192]
        gtok_sb = bigbuf[:, 8192:12288]
        ebc_sb = [bigbuf[:, 12288:12800], bigbuf[:, 13312:13824]]
        emb_sb = [bigbuf[:, 12800:13312], bigbuf[:, 13824:14336]]
        kdecf_sb = bigbuf[:, 14336:14592]
        kdect_sb = bigbuf[:, 14592:14848]
        atm_sb = bigbuf[:, 14848:14976]
        bq = [buf("qe%d" % m) for m in range(8)]
        bk = [buf("kd%d" % m) for m in range(8)]
        bg = [buf("gtok%d" % t4) for t4 in range(4)]
        bebc = [buf("ebc0"), buf("ebc1")]
        bemb = [buf("emb0"), buf("emb1")]
        bkdecf = buf("kdecf")
        bkdect = buf("kdect")
        batm = buf("atm")
        bS = [buf("S%d" % h) for h in range(NH)]
        bSbf = [buf("Sbf%d" % h) for h in range(NH)]
        bxT = [buf("xTs%d" % s) for s in range(NSLAB)]
        bh = buf("hbuf")
        bv = buf("vslab")
        bgt = buf("gslab")
        bogf = buf("ogf")
        beb = buf("eblastT")
        bwvg = [buf("wvg0"), buf("wvg1")]
        bwm = [buf("wm0"), buf("wm1")]
        binner = buf("innerT")
        bltok = buf("ltok")
        blogF = buf("gslab")
        bscr = buf("scratch")
        bscr2n = buf("scratch2")
        brA = buf("scratch")
        brB = buf("rowsB")
        bredc = buf("redc")
        bgTf = buf("gslab")
        bgbc = buf("vslab")
        bxbf_s = [buf("xbf0"), buf("xbf1")]
        bx2bf_s = [buf("x2bf0"), buf("x2bf1")]

        wm_pool = [[wm[:, 0:1024],
                    hbuf[:, 4096:5120], hbuf[:, 5120:6144],
                    hbuf[:, 6144:7168], hbuf[:, 7168:8192]]]
        bwm_pool = [[bwm[0], buf("wmx0"), buf("wmx1"), buf("wmx2"), buf("wmx3")]]
        wvg_half = [wvg[:, 0:2048], wvg[:, 2048:4096]]
        wm_ctr = [0]
        wvg_ctr = [0]

        def stream_lhsT_chunk(dram4, ci):
            pool = wm_pool[0]
            bsel = wm_ctr[0] % len(pool)
            wm_ctr[0] += 1
            dst = pool[bsel]
            bb = bwm_pool[0][bsel]
            G(lambda e: e.dma_start(
                out=dst[:, :].rearrange("p (kt m) -> p kt m", kt=8),
                in_=dram4[ci]),
                writes=[bb])
            return dst, bb

        wvg_pool = [[wvg[:, 0:2048], wvg[:, 2048:4096]]]
        bwvg_pool = [[bwvg[0], bwvg[1]]]

        def stream_wvg_chunk(dram4, ci, kt, m):
            pool = wvg_pool[0]
            bsel = wvg_ctr[0] % len(pool)
            wvg_ctr[0] += 1
            dst = pool[bsel]
            bb = bwvg_pool[0][bsel]
            G(lambda e: e.dma_start(
                out=dst[:, :].rearrange("p (kt m) -> p kt m", kt=kt),
                in_=dram4[ci]),
                writes=[bb])
            return dst, bb

        def ln_slab(s, scale_off, bias_off, dst_c0, dst_stride, pA=None, pAb=None,
                    pB=None, pBb=None):
            if pA is None:
                pA, pAb, pB, pBb = p0, pb["p0"], p1, pb["p1"]
            c0 = s * SLAB
            for d in range(8):
                sl2 = d % 2
                V(lambda e, d=d, c0=c0, sl2=sl2: e.tensor_copy(
                    out=xbf_s[sl2], in_=xT[:, d * N + c0: d * N + c0 + SLAB]),
                    reads=[bxT[s]], writes=[bxbf_s[sl2], bv])
                C(lambda e, d=d, c0=c0, sl2=sl2: e.activation(
                    out=x2bf_s[sl2], in_=xT[:, d * N + c0: d * N + c0 + SLAB],
                    func=AF.Square),
                    reads=[bxT[s]], writes=[bx2bf_s[sl2], bv])
                T(lambda e, d=d, sl2=sl2: e.matmul(pA[0:1, :], ones_col, xbf_s[sl2],
                                                   start=(d == 0), stop=(d == 7)),
                  reads=[bxbf_s[sl2], buf("consb")], writes=[pAb])
                T(lambda e, d=d, sl2=sl2: e.matmul(pB[0:1, :], ones_col, x2bf_s[sl2],
                                                   start=(d == 0), stop=(d == 7)),
                  reads=[bx2bf_s[sl2], buf("consb")], writes=[pBb])
            V(lambda e: e.tensor_single_scalar(out=rowsA, in_=pA[0:1, :],
                                               scalar=1.0 / D, op=ALU.mult),
              reads=[pAb], writes=[brA])
            V(lambda e: e.tensor_tensor(out=rowsB[0:1, :], in0=rowsA, in1=rowsA,
                                        op=ALU.mult),
              reads=[brA], writes=[brB])
            V(lambda e: e.scalar_tensor_tensor(out=rowsB[0:1, :], in0=pB[0:1, :],
                                               scalar=1.0 / D, in1=rowsB[0:1, :],
                                               op0=ALU.mult, op1=ALU.subtract),
              reads=[pBb, brB], writes=[brB])
            C(lambda e: e.activation(out=rowsB[0:1, :], in_=rowsB[0:1, :], func=AF.Sqrt,
                                     bias=cfl[0:1, 0:1]),
              reads=[brB, buf("cfl")], writes=[brB])
            V(lambda e: e.reciprocal(out=rowsB[0:1, :], in_=rowsB[0:1, :]),
              reads=[brB], writes=[brB])
            T(lambda e: e.matmul(pA[:, :], onesrf[0:1, :], rowsA, start=True, stop=True),
              reads=[buf("onesrf"), brA], writes=[pAb])
            T(lambda e: e.matmul(pB[:, :], onesrf[0:1, :], rowsB[0:1, :], start=True, stop=True),
              reads=[buf("onesrf"), brB], writes=[pBb])
            for d in range(8):
                sl2 = d % 2
                scr_s = scratch[:, :] if sl2 == 0 else scratch2[:, :]
                bsc_s = bscr if sl2 == 0 else bscr2n
                V(lambda e, d=d, c0=c0, scr_s=scr_s: e.scalar_tensor_tensor(
                    out=scr_s, in0=pA[:, :], scalar=-1.0,
                    in1=xT[:, d * N + c0: d * N + c0 + SLAB], op0=ALU.mult, op1=ALU.add),
                    reads=[pAb, bxT[s]], writes=[bsc_s])
                V(lambda e, d=d, scr_s=scr_s: e.tensor_tensor(
                    out=scr_s, in0=scr_s, in1=pB[:, :], op=ALU.mult),
                    reads=[bsc_s, pBb], writes=[bsc_s])
                C(lambda e, d=d, dst_c0=dst_c0, dst_stride=dst_stride, scr_s=scr_s: e.activation(
                    out=hbuf[:, d * dst_stride + dst_c0: d * dst_stride + dst_c0 + SLAB],
                    in_=scr_s, func=AF.Identity,
                    scale=modc[:, scale_off + d:scale_off + d + 1],
                    bias=modc[:, bias_off + d:bias_off + d + 1]),
                    reads=[bsc_s, buf("modc")], writes=[bh])

        qk_pool = [(p0, pb["p0"]), (p1, pb["p1"]), (p4, pb["p4"]), (p5, pb["p5"])]

        def proj_slab_featmaj(wdram, m, pbank, pbuf):
            wb, wbb = stream_lhsT_chunk(wdram, m)
            for k in range(8):
                T(lambda e, k=k, wb=wb, pbank=pbank: e.matmul(
                    pbank[:, :], wb[:, k * 128:(k + 1) * 128],
                    hbuf[:, k * 512:(k + 1) * 512],
                    start=(k == 0), stop=(k == 7)),
                    reads=[wbb, bh], writes=[pbuf])

        for s in range(NSLAB):
            c0 = s * SLAB
            ln_slab(s, 8, 0, 0, 512, p2, pb["p2"], p3, pb["p3"])
            for k in range(8):
                T(lambda e, k=k: e.matmul(p0[0:32, :], wgk1T[:, k * 32:(k + 1) * 32],
                                          hbuf[:, k * 512:(k + 1) * 512],
                                          start=(k == 0), stop=(k == 7)),
                  reads=[buf("wgk1T"), bh], writes=[pb["p0"]])
            V(lambda e: e.tensor_copy(out=innerT[0:32, 0:512], in_=p0[0:32, :]),
              reads=[pb["p0"]], writes=[binner])
            if s == 0:
                V(lambda e: e.memset(innerT[32:33, 0:512], 1.0), writes=[binner])
            for t4 in range(4):
                for nn in range(2):
                    gp, gpb = (p1, pb["p1"]) if (t4 * 2 + nn) % 2 == 0 else (p4, pb["p4"])
                    T(lambda e, t4=t4, nn=nn, gp=gp: e.matmul(
                        gp[:, :], innerT[:, t4 * 128:(t4 + 1) * 128],
                        wgk2Tb[:, nn * 512:(nn + 1) * 512], start=True, stop=True),
                        reads=[binner, buf("wgk2Tb")], writes=[gpb])
                    C(lambda e, t4=t4, nn=nn, gp=gp: e.activation(
                        out=gtok_sb[:, t4 * 1024 + nn * 512: t4 * 1024 + (nn + 1) * 512],
                        in_=gp[:, :], func=AF.Sigmoid),
                        reads=[gpb], writes=[bg[t4]])
            for t4 in range(4):
                C(lambda e, t4=t4: e.activation(
                    out=gtok_sb[:, t4 * 1024: (t4 + 1) * 1024],
                    in_=gtok_sb[:, t4 * 1024: (t4 + 1) * 1024], func=AF.Ln),
                    reads=[bg[t4]], writes=[bg[t4]])
            for dp in range(4):
                for j, pbk, pbf in ((0, p2, pb["p2"]), (1, p3, pb["p3"])):
                    d = 2 * dp + j
                    for t4 in range(4):
                        T(lambda e, d=d, t4=t4, pbk=pbk: e.matmul(
                            pbk[:, t4 * 128:(t4 + 1) * 128],
                            gtok_sb[:, t4 * 1024 + d * 128: t4 * 1024 + (d + 1) * 128],
                            UTneg, start=True, stop=True),
                            reads=[bg[t4], buf("consb")], writes=[pbf])
                    C(lambda e, j=j, pbk=pbk: e.activation(out=ebc_sb[j][:, :], in_=pbk[:, :],
                                                           func=AF.Exp, bias=cfl[:, 1:2]),
                      reads=[pbf, buf("cfl")], writes=[bebc[j]])
                    C(lambda e, j=j, pbk=pbk: e.activation(out=emb_sb[j][:, :], in_=pbk[:, :],
                                                           func=AF.Exp, scale=cfl[:, 2:3]),
                      reads=[pbf, buf("cfl")], writes=[bemb[j]])
                    V(lambda e, d=d, j=j, s=s: e.tensor_single_scalar(
                        out=eblastT[:, d * 16 + s * 4: d * 16 + s * 4 + 4]
                        .rearrange("p (c m) -> p c m", m=1),
                        in_=ebc_sb[j].rearrange("p (c m) -> p c m", m=128)[:, :, 127:128],
                        scalar=16.0, op=ALU.mult),
                        reads=[bebc[j]], writes=[beb])
                for j in (0, 1):
                    d = 2 * dp + j
                    qp, qpb = qk_pool[(2 * d) % 4]
                    proj_slab_featmaj(wqT_d, d, qp, qpb)
                    V(lambda e, d=d, j=j, qp=qp: e.tensor_tensor(
                        out=qe_sb[:, d * 512:(d + 1) * 512], in0=qp[:, :], in1=ebc_sb[j][:, :],
                        op=ALU.mult),
                        reads=[qpb, bebc[j]], writes=[bq[d]])
                    kp, kpb = qk_pool[(2 * d + 1) % 4]
                    proj_slab_featmaj(wkT_d, d, kp, kpb)
                    V(lambda e, d=d, j=j, kp=kp: e.tensor_tensor(
                        out=kd_sb[:, d * 512:(d + 1) * 512], in0=kp[:, :], in1=emb_sb[j][:, :],
                        op=ALU.mult),
                        reads=[kpb, bemb[j]], writes=[bk[d]])
            vg_pool = [(p2, pb["p2"]), (p3, pb["p3"]), (p0, pb["p0"]), (p1, pb["p1"])]
            wvg_pool[0] = [wvg[:, 0:2048], wvg[:, 2048:4096],
                           ogbuf[:, 0:2048], ogbuf[:, 2048:4096]]
            bwvg_pool[0] = [bwvg[0], bwvg[1], bogf, bogf]
            for half, wdram_, dstbuf, dstb, actf in (
                    (0, wvT_d, vslab, bv, None), (1, wgT_d, gslab, bgt, AF.Silu)):
                for nv in range(8):
                    wb, wbb = stream_wvg_chunk(wdram_, nv, 8, 256)
                    for t4 in range(4):
                        pbk, pbf = vg_pool[(nv * 4 + t4) % 4]
                        for k in range(8):
                            T(lambda e, k=k, t4=t4, wb=wb, pbk=pbk: e.matmul(
                                pbk[:, 0:256],
                                hbuf[:, k * 512 + t4 * 128: k * 512 + t4 * 128 + 128],
                                wb[:, k * 256:(k + 1) * 256],
                                start=(k == 0), stop=(k == 7)),
                                reads=[wbb, bh], writes=[pbf])
                        if actf is None:
                            V(lambda e, nv=nv, t4=t4, pbk=pbk, dstbuf=dstbuf: e.tensor_copy(
                                out=dstbuf[:, t4 * 2048 + nv * 256: t4 * 2048 + (nv + 1) * 256],
                                in_=pbk[:, 0:256]),
                                reads=[pbf], writes=[dstb])
                        else:
                            C(lambda e, nv=nv, t4=t4, pbk=pbk, dstbuf=dstbuf, actf=actf:
                              e.activation(
                                  out=dstbuf[:, t4 * 2048 + nv * 256: t4 * 2048 + (nv + 1) * 256],
                                  in_=pbk[:, 0:256], func=actf),
                                reads=[pbf], writes=[dstb])
                            V(lambda e, nv=nv, t4=t4, dstbuf=dstbuf: e.tensor_tensor(
                                out=dstbuf[:, t4 * 2048 + nv * 256: t4 * 2048 + (nv + 1) * 256],
                                in0=dstbuf[:, t4 * 2048 + nv * 256: t4 * 2048 + (nv + 1) * 256],
                                in1=rmswbc[:, (nv % 2) * 256: (nv % 2) * 256 + 256],
                                op=ALU.mult),
                                reads=[dstb, buf("rmswbc")], writes=[dstb])
            wvg_pool[0] = [wvg[:, 0:2048], wvg[:, 2048:4096]]
            bwvg_pool[0] = [bwvg[0], bwvg[1]]
            pend_tr = []
            for t4 in range(4):
                tau = s * 4 + t4
                vt = vslab[:, t4 * 2048: (t4 + 1) * 2048]
                gt = gslab[:, t4 * 2048: (t4 + 1) * 2048]
                for h in range(NH):
                    while len(pend_tr) > 3:
                        pend_tr.pop(0)()
                    qcols = [(2 * h + j) * 512 + t4 * 128 for j in (0, 1)]
                    if tau > 0:
                        for j in (0, 1):
                            T(lambda e, j=j, qcols=qcols, h=h: e.matmul(
                                p4[:, :], qe_sb[:, qcols[j]:qcols[j] + 128],
                                S_bf[:, (2 * h + j) * 512:(2 * h + j + 1) * 512],
                                start=(j == 0), stop=False),
                                reads=[bq[2 * h + j], bSbf[h]], writes=[pb["p4"]])
                    for j in (0, 1):
                        T(lambda e, j=j, qcols=qcols: e.matmul(
                            p5[:, 0:128], kd_sb[:, qcols[j]:qcols[j] + 128],
                            qe_sb[:, qcols[j]:qcols[j] + 128],
                            start=(j == 0), stop=(j == 1)),
                            reads=[bk[2 * h + j], bq[2 * h + j]], writes=[pb["p5"]])
                    V(lambda e: e.tensor_tensor(out=atm_sb[:, :], in0=p5[:, 0:128],
                                                in1=UTmask, op=ALU.mult),
                      reads=[pb["p5"], buf("consb")], writes=[batm])
                    T(lambda e, h=h, tau=tau, vt=vt: e.matmul(
                        p4[:, :], atm_sb[:, :], vt[:, h * 512:(h + 1) * 512],
                        start=(tau == 0), stop=True),
                        reads=[batm, bv], writes=[pb["p4"]])
                    for j in (0, 1):
                        T_eb = eblastT[:, (2 * h + j) * 16 + tau: (2 * h + j) * 16 + tau + 1]
                        C(lambda e, j=j, qcols=qcols, T_eb=T_eb: e.mul(
                            out=kdecf_sb[:, j * 128:(j + 1) * 128],
                            in_=kd_sb[:, qcols[j]:qcols[j] + 128], mul=T_eb),
                            reads=[bk[2 * h + j], beb], writes=[bkdecf])
                        T(lambda e, j=j: e.transpose(
                            pT[:, j * 128:(j + 1) * 128], kdecf_sb[:, j * 128:(j + 1) * 128],
                            identb),
                            reads=[bkdecf, buf("consb")], writes=[pb["pT"]])
                    C(lambda e: e.copy(out=kdect_sb[:, :], in_=pT[:, 0:256]),
                      reads=[pb["pT"]], writes=[bkdect])
                    for j, pbk, pbf in ((0, p2, pb["p2"]), (1, p3, pb["p3"])):
                        T(lambda e, j=j, h=h, pbk=pbk, vt=vt: e.matmul(
                            pbk[:, :], kdect_sb[:, j * 128:(j + 1) * 128],
                            vt[:, h * 512:(h + 1) * 512], start=True, stop=True),
                            reads=[bkdect, bv], writes=[pbf])
                    C(lambda e: e.activation(out=scratch[:, :], in_=p4[:, :], func=AF.Square,
                                             accum_out=redc[:, 0:1]),
                      reads=[pb["p4"]], writes=[bscr, bredc])
                    C(lambda e: e.activation(out=redc[:, 1:2], in_=redc[:, 0:1], func=AF.Sqrt,
                                             scale=cfl[:, 4:5], bias=cfl[:, 0:1]),
                      reads=[bredc, buf("cfl")], writes=[bredc])
                    V(lambda e: e.reciprocal(out=redc[:, 2:3], in_=redc[:, 1:2]),
                      reads=[bredc], writes=[bredc])
                    C(lambda e: e.activation(out=scratch[:, :], in_=p4[:, :], func=AF.Identity,
                                             scale=redc[:, 2:3]),
                      reads=[pb["p4"], bredc], writes=[bscr])
                    V(lambda e, h=h, gt=gt: e.tensor_tensor(
                        out=gt[:, h * 512:(h + 1) * 512], in0=scratch[:, :],
                        in1=gt[:, h * 512:(h + 1) * 512], op=ALU.mult),
                        reads=[bscr, bgt], writes=[bgt])

                    def emit_og_transposes(h=h, t4=t4, gt=gt):
                        pp, ppb = pU, pb["pU"]
                        for ci in range(4):
                            cc = h * 4 + ci
                            T(lambda e, cc=cc, ci=ci, pp=pp, gt=gt: e.transpose(
                                pp[:, ci * 128:(ci + 1) * 128],
                                gt[:, cc * 128:(cc + 1) * 128], identb),
                                reads=[bgt, buf("consb")], writes=[ppb])
                        V(lambda e, h=h, t4=t4, pp=pp: e.tensor_copy(
                            out=ogbuf[:, :].rearrange("p (c m) -> p c m", m=512)
                            [:, h * 4:(h + 1) * 4, t4 * 128:(t4 + 1) * 128],
                            in_=pp[:, 0:512].rearrange("p (c m) -> p c m", m=128)),
                            reads=[ppb], writes=[bogf])
                    pend_tr.append(emit_og_transposes)
                    for j, pbk, pbf in ((0, p2, pb["p2"]), (1, p3, pb["p3"])):
                        scol = (2 * h + j) * 512
                        T_eb = eblastT[:, (2 * h + j) * 16 + tau: (2 * h + j) * 16 + tau + 1]
                        if tau == 0:
                            V(lambda e, scol=scol, pbk=pbk: e.tensor_copy(
                                out=S[:, scol:scol + 512], in_=pbk[:, :]),
                                reads=[pbf], writes=[bS[h]])
                        else:
                            V(lambda e, scol=scol, pbk=pbk, T_eb=T_eb: e.scalar_tensor_tensor(
                                out=S[:, scol:scol + 512], in0=S[:, scol:scol + 512],
                                scalar=T_eb, in1=pbk[:, :], op0=ALU.mult, op1=ALU.add),
                                reads=[bS[h], beb, pbf], writes=[bS[h]])
                    C(lambda e, h=h: e.copy(out=S_bf[:, h * 1024:(h + 1) * 1024],
                                            in_=S[:, h * 1024:(h + 1) * 1024]),
                      reads=[bS[h]], writes=[bSbf[h]])


            while pend_tr:
                pend_tr.pop(0)()
            for m in range(8):
                for kh in (0, 1):
                    wb, wbb = stream_lhsT_chunk(woutT_d[m], kh)
                    for k in range(8):
                        kk = kh * 8 + k
                        T(lambda e, kk=kk, k=k, wb=wb: e.matmul(
                            p0[:, :], wb[:, k * 128:(k + 1) * 128],
                            ogbuf[:, kk * 512:(kk + 1) * 512],
                            start=(kk == 0), stop=(kk == 15)),
                            reads=[wbb, bogf], writes=[pb["p0"]])
                V(lambda e, m=m, c0=c0, s=s: e.scalar_tensor_tensor(
                    out=xT[:, m * N + c0: m * N + c0 + SLAB], in0=p0[:, :],
                    scalar=modc[:, 16 + m:16 + m + 1],
                    in1=xT[:, m * N + c0: m * N + c0 + SLAB], op0=ALU.mult, op1=ALU.add),
                    reads=[pb["p0"], buf("modc"), bxT[s]], writes=[bxT[s]])

        sch.barrier()
        w1big = [vslab[:, 4096:6144], vslab[:, 6144:8192]]
        bw1big = [buf("wmf0"), buf("wmf1")]
        wm_pool[0] = [wm[:, 0:1024]]
        bwm_pool[0] = [bwm[0]]

        SY(lambda e: e.dma_start(out=selmat, in_=selmat_d), writes=[buf("selmat")],
           deps=())
        stores = []
        for th in range(2):
            for sl2 in range(2):
                ln_slab(th * 2 + sl2, 32, 24, sl2 * 512, 1024, p2, pb["p2"], p3, pb["p3"])
            for sl2 in range(2):
                for k in range(8):
                    T(lambda e, k=k, sl2=sl2: e.matmul(
                        p1[0:8, :], routerT[:, k * 8:(k + 1) * 8],
                        hbuf[:, k * 1024 + sl2 * 512: k * 1024 + sl2 * 512 + 512],
                        start=(k == 0), stop=(k == 7)),
                        reads=[buf("routerT"), bh], writes=[pb["p1"]])
                V(lambda e: e.tensor_copy(out=logitsF, in_=p1[0:8, :]),
                  reads=[pb["p1"]], writes=[blogF])
                for t4 in range(4):
                    T(lambda e, t4=t4: e.transpose(
                        pT[:, 0:8], logitsF[:, t4 * 128:(t4 + 1) * 128], identb[0:8, 0:8]),
                        reads=[blogF, buf("consb")], writes=[pb["pT"]])
                    tt = th * 8 + sl2 * 4 + t4
                    V(lambda e, tt=tt: e.tensor_copy(out=ltok[:, tt * 8:(tt + 1) * 8],
                                                     in_=pT[:, 0:8]),
                      reads=[pb["pT"]], writes=[bltok])
            def emit_gating(th=th):
                for tt in range(th * 8, th * 8 + 8):
                    lsl = ltok[:, tt * 8:(tt + 1) * 8]
                    r = gsc[:, (tt % 4) * 32:(tt % 4) * 32 + 32]
                    bt = buf("gsc%d" % (tt % 4))
                    m1 = r[:, 24:25]
                    m2 = r[:, 25:26]
                    d21 = r[:, 26:27]
                    den = r[:, 27:28]
                    rc_ = r[:, 28:29]
                    V(lambda e, lsl=lsl, m1=m1: e.reduce_max(out=m1, in_=lsl, axis=AX.X),
                      reads=[bltok], writes=[bt])
                    V(lambda e, lsl=lsl, r=r, m1=m1: e.tensor_single_scalar(
                        out=r[:, 0:8], in_=lsl, scalar=m1, op=ALU.subtract),
                      reads=[bltok, bt], writes=[bt])
                    C(lambda e, r=r: e.activation(out=r[:, 8:16], in_=r[:, 0:8], func=AF.Exp),
                      reads=[bt], writes=[bt])
                    V(lambda e, lsl=lsl, r=r, m1=m1: e.tensor_single_scalar(
                        out=r[:, 16:24], in_=lsl, scalar=m1, op=ALU.is_ge),
                      reads=[bltok, bt], writes=[bt])
                    V(lambda e, lsl=lsl, r=r: e.scalar_tensor_tensor(
                        out=r[:, 16:24], in0=r[:, 16:24], scalar=-1e30,
                        in1=lsl, op0=ALU.mult, op1=ALU.add),
                      reads=[bt, bltok], writes=[bt])
                    V(lambda e, r=r, m2=m2: e.reduce_max(out=m2, in_=r[:, 16:24], axis=AX.X),
                      reads=[bt], writes=[bt])
                    V(lambda e, m1=m1, m2=m2, d21=d21: e.tensor_tensor(
                        out=d21, in0=m2, in1=m1, op=ALU.subtract),
                      reads=[bt], writes=[bt])
                    C(lambda e, d21=d21, den=den: e.activation(out=den, in_=d21, func=AF.Exp),
                      reads=[bt], writes=[bt])
                    V(lambda e, den=den: e.tensor_scalar_add(den, den, 1.0),
                      reads=[bt], writes=[bt])
                    V(lambda e, den=den, rc_=rc_: e.reciprocal(out=rc_, in_=den),
                      reads=[bt], writes=[bt])
                    V(lambda e, lsl=lsl, r=r, m2=m2: e.tensor_single_scalar(
                        out=r[:, 0:8], in_=lsl, scalar=m2, op=ALU.is_ge),
                      reads=[bltok, bt], writes=[bt])
                    V(lambda e, r=r, rc_=rc_: e.tensor_single_scalar(
                        out=r[:, 8:16], in_=r[:, 8:16], scalar=rc_, op=ALU.mult),
                      reads=[bt], writes=[bt])
                    V(lambda e, tt=tt, r=r: e.tensor_tensor(
                        out=ltok[:, tt * 8:(tt + 1) * 8], in0=r[:, 8:16],
                        in1=r[:, 0:8], op=ALU.mult),
                      reads=[bt], writes=[bltok])
                for tt in range(th * 8, th * 8 + 8):
                    T(lambda e, tt=tt: e.transpose(p2[0:8, 0:128], ltok[:, tt * 8:(tt + 1) * 8],
                                                   identf[:, :]),
                      reads=[bltok, buf("identf")], writes=[pb["p2"]])
                    V(lambda e, tt=tt: e.tensor_copy(out=gateTf[:, tt * 128:(tt + 1) * 128],
                                                     in_=p2[0:8, 0:128]),
                      reads=[pb["p2"]], writes=[bgTf])

            w1_pool = [(p0, pb["p0"]), (p1, pb["p1"]), (p2, pb["p2"]), (p3, pb["p3"])]
            w2_pool = [(p4, pb["p4"]), (p5, pb["p5"]), (p2, pb["p2"]), (p3, pb["p3"])]
            for e_i in range(NE):
                def emit_gate_bc(e_i=e_i, th=th):
                    for sl in range(2):
                        T(lambda e, e_i=e_i, sl=sl, th=th: e.matmul(
                            p4[:, :], selmat[:, e_i * 128:(e_i + 1) * 128],
                            gateTf[:, th * 1024 + sl * 512: th * 1024 + (sl + 1) * 512],
                            start=True, stop=True),
                            reads=[buf("selmat"), bgTf], writes=[pb["p4"]])
                        V(lambda e, sl=sl: e.tensor_copy(
                            out=gate_bc[:, sl * 512:(sl + 1) * 512], in_=p4[:, :]),
                          reads=[pb["p4"]], writes=[bgbc])
                defer = (e_i == 0)
                pending_mults = []
                if not defer:
                    emit_gate_bc()
                for m2 in range(8):
                    if defer and m2 == 4:
                        emit_gating()
                        emit_gate_bc()
                        for pm in pending_mults:
                            pm()
                        pending_mults = []
                        defer = False
                    bsel = m2 % 2
                    wb, wbb = w1big[bsel], bw1big[bsel]
                    G(lambda e, e_i=e_i, m2=m2, wb=wb: e.dma_start(
                        out=wb.rearrange("p (mm kt m) -> p mm kt m", mm=2, kt=8),
                        in_=w1_d[e_i * 16 + m2 * 2: e_i * 16 + m2 * 2 + 2]
                        .rearrange("mm p kt m -> p mm kt m")),
                        writes=[wbb])
                    for mi in range(2):
                        m = m2 * 2 + mi
                        for k in range(8):
                            for n2 in range(2):
                                pbk, pbf = w1_pool[(m % 2) * 2 + n2]
                                T(lambda e, k=k, n2=n2, mi=mi, wb=wb, pbk=pbk: e.matmul(
                                    pbk[:, :], wb[:, mi * 1024 + k * 128: mi * 1024 + (k + 1) * 128],
                                    hbuf[:, k * 1024 + n2 * 512: k * 1024 + n2 * 512 + 512],
                                    start=(k == 0), stop=(k == 7)),
                                    reads=[wbb, bh], writes=[pbf])
                        for n2 in range(2):
                            pbk, pbf = w1_pool[(m % 2) * 2 + n2]
                            C(lambda e, m=m, n2=n2, pbk=pbk: e.activation(
                                out=bigbuf[:, m * 1024 + n2 * 512: m * 1024 + (n2 + 1) * 512],
                                in_=pbk[:, :], func=AF.Gelu),
                                reads=[pbf], writes=[buf("he")])
                        def emit_mult(m=m):
                            V(lambda e, m=m: e.tensor_tensor(
                                out=bigbuf[:, m * 1024: (m + 1) * 1024],
                                in0=bigbuf[:, m * 1024: (m + 1) * 1024],
                                in1=gate_bc[:, 0:1024], op=ALU.mult),
                                reads=[buf("he"), bgbc], writes=[buf("he")])
                        if defer:
                            pending_mults.append(emit_mult)
                        else:
                            emit_mult()
                wvg_pool[0] = [wvg[:, 0:2048], wvg[:, 2048:4096],
                               gslab[:, 4096:6144], gslab[:, 6144:8192]]
                bwvg_pool[0] = [bwvg[0], bwvg[1], buf("w2x0"), buf("w2x1")]
                for md in range(8):
                    wb, wbb = stream_wvg_chunk(w2_d[e_i], md, 16, 128)
                    for k in range(16):
                        for n2 in range(2):
                            pbk, pbf = w2_pool[(md % 2) * 2 + n2]
                            T(lambda e, k=k, n2=n2, wb=wb, pbk=pbk: e.matmul(
                                pbk[:, :], wb[:, k * 128:(k + 1) * 128],
                                bigbuf[:, k * 1024 + n2 * 512: k * 1024 + n2 * 512 + 512],
                                start=(k == 0), stop=(k == 15)),
                                reads=[wbb, buf("he")], writes=[pbf])
                    for n2 in range(2):
                        pbk, pbf = w2_pool[(md % 2) * 2 + n2]
                        tc0 = th * 1024 + n2 * 512
                        sx = th * 2 + n2
                        V(lambda e, md=md, tc0=tc0, pbk=pbk, sx=sx: e.scalar_tensor_tensor(
                            out=xT[:, md * N + tc0: md * N + tc0 + 512], in0=pbk[:, :],
                            scalar=modc[:, 40 + md:40 + md + 1],
                            in1=xT[:, md * N + tc0: md * N + tc0 + 512],
                            op0=ALU.mult, op1=ALU.add),
                            reads=[pbf, buf("modc"), bxT[sx]], writes=[bxT[sx]])

            stores.append(SY(lambda e, th=th: e.dma_start(
                out=y_d.rearrange("(kt p) t -> p kt t", p=128)[
                    :, :, th * 1024:(th + 1) * 1024],
                in_=xT[:, :].rearrange("p (kt t) -> p kt t", kt=8)[
                    :, :, th * 1024:(th + 1) * 1024]),
                reads=[bxT[th * 2], bxT[th * 2 + 1]]))

        C(lambda e: e.nop(), deps=stores)

        sch.lower(nc)
    return nc


_CACHE = {}


def kernel(**inp):
    f32 = np.float32
    x = np.asarray(inp["L0_attn_x"], dtype=f32)
    Wq = np.asarray(inp["L0_attn_Wq"], dtype=f32)
    Wk = np.asarray(inp["L0_attn_Wk"], dtype=f32)
    Wv = np.asarray(inp["L0_attn_Wv"], dtype=f32)
    Wg = np.asarray(inp["L0_attn_Wg"], dtype=f32)
    Wgk1 = np.asarray(inp["L0_attn_Wgk1"], dtype=f32)
    Wgk2 = np.asarray(inp["L0_attn_Wgk2"], dtype=f32)
    bgk2 = np.asarray(inp["L0_attn_bgk2"], dtype=f32)
    Wout = np.asarray(inp["L0_attn_Wout"], dtype=f32)
    rms_w = np.asarray(inp["L0_attn_rms_w"], dtype=f32)
    router_w = np.asarray(inp["L0_ffn_router_w"], dtype=f32)
    w1 = np.asarray(inp["L0_ffn_w1"], dtype=f32)
    w2 = np.asarray(inp["L0_ffn_w2"], dtype=f32)
    Cc = np.asarray(inp["L0_c"], dtype=f32)
    wmod = np.asarray(inp["L0_wmod"], dtype=f32)

    cc = np.ascontiguousarray
    WqTt = cc(Wq.T.reshape(8, 128, 8, 128).transpose(2, 1, 0, 3))
    WkTt = cc(Wk.T.reshape(8, 128, 8, 128).transpose(2, 1, 0, 3))
    WvTt = cc(Wv.T.reshape(8, 128, 8, 256).transpose(2, 1, 0, 3))
    WgTt = cc(Wg.T.reshape(8, 128, 8, 256).transpose(2, 1, 0, 3))
    Wgk1Tt = cc(Wgk1.T.reshape(8, 128, 32).transpose(1, 0, 2))
    Wgk2Tb = cc(np.concatenate([Wgk2.T, bgk2[None, :]], axis=0))
    WoutTt = cc(Wout.T.reshape(2, 8, 128, 8, 128).transpose(3, 0, 2, 1, 4))
    w1t = cc(w1.reshape(8, 128, 128, 128).transpose(2, 1, 0, 3))
    w2t = cc(w2.reshape(8, 16, 128, 8, 128).transpose(0, 3, 2, 1, 4))
    routerTt = cc(router_w.T.reshape(8, 128, 8).transpose(1, 0, 2).reshape(128, 64))
    CTt = cc(Cc.T.reshape(8, 128, 8).transpose(1, 0, 2).reshape(128, 64))
    rmswRow = cc(rms_w[None, :])

    bf = ml_dtypes.bfloat16
    UTneg = np.triu(np.full((128, 128), 1.0 / 32.0, dtype=f32)).astype(bf)
    UTmask = np.triu(np.ones((128, 128), dtype=f32)).astype(bf)
    identb = np.eye(128, dtype=f32).astype(bf)
    consBf = np.zeros((128, 512), dtype=bf)
    consBf[:, 0:128] = UTneg
    consBf[:, 128:256] = UTmask
    consBf[:, 256:384] = identb
    consBf[:, 384] = np.ones(128, dtype=bf)
    onesRowBf = np.ones((1, 128), dtype=bf)
    onesRowF = np.ones((1, 128), dtype=f32)
    identF = np.eye(128, dtype=f32)
    selMat = np.zeros((8, 1024), dtype=bf)
    for e_ in range(8):
        selMat[e_, e_ * 128:(e_ + 1) * 128] = 1.0
    cflC = np.zeros((128, 8), dtype=f32)
    cflC[:, 0] = 1e-5
    cflC[:, 1] = -np.log(16.0)
    cflC[:, 2] = -1.0
    cflC[:, 3] = 1.0 / 1024.0
    cflC[:, 4] = 1.0 / 512.0
    cflC[:, 5] = 16.0
    cflC[:, 6] = 0.25
    cflC[:, 7] = -1e30

    shared = {
        "WqTt": WqTt, "WkTt": WkTt, "WvTt": WvTt, "WgTt": WgTt, "Wgk1Tt": Wgk1Tt,
        "Wgk2Tb": Wgk2Tb, "WoutTt": WoutTt, "routerTt": routerTt, "w1t": w1t, "w2t": w2t,
        "CTt": CTt, "wmod": wmod, "rmswRow": rmswRow, "consBf": consBf,
        "onesRowBf": onesRowBf, "onesRowF": onesRowF, "identF": identF,
        "selMat": selMat, "cflC": cflC,
    }
    in_maps = []
    for b in range(8):
        m = dict(shared)
        m["xTt"] = cc(x[b].T.reshape(8, 128, 2048).transpose(1, 0, 2))
        ct = CTt.reshape(128, 8, 8)
        m["CTt"] = cc(np.concatenate([ct[:, :, b:b + 1], ct[:, :, b + 1:], ct[:, :, :b]],
                                     axis=2).reshape(128, 64))
        in_maps.append(m)

    nc = _CACHE.get("nc")
    if nc is None:
        nc = build_program(0)
        _CACHE["nc"] = nc

    res = run_bass_kernel_spmd(nc, in_maps, core_ids=list(range(8)))
    out = np.empty((8, N, D), dtype=np.float32)
    for b in range(8):
        out[b] = np.asarray(res.results[b]["y"]).T
    return out
